# version 20 (final)
# speedup vs baseline: 2.3400x; 1.0280x over previous
import numpy as np
import concourse.bass as bass
import concourse.mybir as mybir
from concourse.bass_utils import run_bass_kernel_spmd

F32 = mybir.dt.float32
F16 = mybir.dt.float16
Alu = mybir.AluOpType
Act = mybir.ActivationFunctionType
AX = mybir.AxisListType.X

B, S, D = 4, 4096, 1024
Q = 2048
NHP = 8
ZINIT = 0.67448975
MAGIC = 12582912.0
LN_EPS = 1e-5

ENGINES = ("sync", "vector", "scalar", "tensor", "gpsimd")


class Prog:
    def __init__(self):
        self.streams = {e: [] for e in ENGINES}
        self.sems = {}
        self.handles = {}

    def bind(self, mapping):
        for n, h in mapping.items():
            self.handles[n] = h
            self.sems[n] = 0

    def on(self, eng, fn, wait=(), inc=None, inc_n=1):
        wl = []
        for w in wait:
            if isinstance(w, tuple):
                if w[1] > 0:
                    wl.append((self.handles[w[0]], w[1]))
            elif self.sems[w] > 0:
                wl.append((self.handles[w], self.sems[w]))
        it = None
        if inc is not None:
            self.sems[inc] += inc_n
            it = (self.handles[inc], self.sems[inc] - self.sems[inc] + inc_n)
            it = (self.handles[inc], inc_n)
        self.streams[eng].append((wl, fn, it))

    def cnt(self, name):
        return self.sems[name]

    def now(self, *names):
        return [(n, self.sems[n]) for n in names]

    def barrier(self, eng, names):
        wl = [(self.handles[n], self.sems[n]) for n in names
              if self.sems[n] > 0]
        if wl:
            self.streams[eng].append((wl, None, None))

    def run(self, block):
        def mk(ename):
            def body(eng):
                for wl, fn, it in self.streams[ename]:
                    for h, v in wl:
                        eng.wait_ge(h, v)
                    inst = fn(eng) if fn is not None else None
                    if it is not None:
                        inst.then_inc(it[0], it[1])
            return body
        block.sync(mk("sync"))
        block.vector(mk("vector"))
        block.scalar(mk("scalar"))
        block.tensor(mk("tensor"))
        block.gpsimd(mk("gpsimd"))


def build_program(debug=False):
    nc = bass.Bass("TRN2", target_bir_lowering=False, debug=False)

    xkv = nc.dram_tensor("xkv", [S, D], F32, kind="ExternalInput").ap()
    xq_d = nc.dram_tensor("xq", [Q, D], F32, kind="ExternalInput").ap()
    Wq = nc.dram_tensor("Wq", [D, D], F32, kind="ExternalInput").ap()
    Wk = nc.dram_tensor("Wk", [D, D], F32, kind="ExternalInput").ap()
    Wv = nc.dram_tensor("Wv", [D, D], F32, kind="ExternalInput").ap()
    Wo_d = nc.dram_tensor("Wo", [D, D], F32, kind="ExternalInput").ap()
    Wg_d = nc.dram_tensor("Wg", [D, D], F32, kind="ExternalInput").ap()
    bias_d = {}
    for nm in ("bq", "bk", "bv", "bo", "bg", "b2"):
        bias_d[nm] = nc.dram_tensor(nm, [D], F32, kind="ExternalInput").ap()
    b1_d = nc.dram_tensor("b1", [4 * D], F32, kind="ExternalInput").ap()
    Wt_d = nc.dram_tensor("Wt", [D, 1], F32, kind="ExternalInput").ap()
    bt_d = nc.dram_tensor("bt", [1], F32, kind="ExternalInput").ap()
    W1_d = nc.dram_tensor("W1", [4 * D, D], F32, kind="ExternalInput").ap()
    W2_d = nc.dram_tensor("W2", [D, 4 * D], F32, kind="ExternalInput").ap()
    y_d = nc.dram_tensor("y", [Q, D], F32, kind="ExternalOutput").ap()
    scr_h = nc.dram_tensor("scr", [256], F32, kind="Internal").ap()
    dbg = {}
    if debug:
        for nm, shape, dt_ in (
                ("dbg_xtln", [128, 4096], F16), ("dbg_tchain", [1, 8], F32),
                ("dbg_kn", [128, 4096], F16), ("dbg_qn", [128, 2048], F16),
                ("dbg_vtm", [128, 4224], F16), ("dbg_tb", [128, 4096], F16),
                ("dbg_otmp", [65, 512], F16), ("dbg_oall", [128, 16384], F16),
                ("dbg_owo", [128, 16384], F16), ("dbg_x2", [128, 16384], F32),
                ("dbg_m2h", [64, 64], F16), ("dbg_ksum", [128, 1], F16),
                ("dbg_strow", [1, 512], F32), ("dbg_eg", [128, 512], F16),
                ("dbg_rs", [128, 2], F32), ("dbg_x2b", [128, 8192], F32),
                ("dbg_x2l", [128, 8192], F16), ("dbg_w1s", [128, 1024], F16)):
            dbg[nm] = nc.dram_tensor(nm, shape, dt_, kind="ExternalOutput").ap()
    w1tT_h = nc.dram_tensor("w1tT", [32, 128, 8, 128], F16, kind="Internal").ap()
    w2tT_h = nc.dram_tensor("w2tT", [8, 128, 32, 128], F16, kind="Internal").ap()

    ARN = 101248

    import contextlib
    with contextlib.ExitStack() as _es:
        ident = _es.enter_context(nc.sbuf_tensor("ident", [128, 128], F16))
        ones_r = _es.enter_context(nc.sbuf_tensor("ones_r", [1, 128], F16))
        ones_c = _es.enter_context(nc.sbuf_tensor("ones_c", [128, 1], F16))
        bd2 = _es.enter_context(nc.sbuf_tensor("bd2", [128, 2], F16))
        bd2T = _es.enter_context(nc.sbuf_tensor("bd2T", [2, 128], F16))
        bias5 = _es.enter_context(nc.sbuf_tensor("bias5", [128, 48], F32))
        b1s = _es.enter_context(nc.sbuf_tensor("b1s", [128, 32], F32))
        wtrow = _es.enter_context(nc.sbuf_tensor("wtrow", [1, 1024], F32))
        tmp1 = _es.enter_context(nc.sbuf_tensor("tmp1", [1, 1024], F32))
        small = _es.enter_context(nc.sbuf_tensor("small", [128, 16], F32))
        cst = _es.enter_context(nc.sbuf_tensor("cst", [128, 8], F32))
        ones32 = _es.enter_context(nc.sbuf_tensor("ones32", [1, 128], F32))
        ones_f = _es.enter_context(nc.sbuf_tensor("ones_f", [128, 128], F16))
        tchain = _es.enter_context(nc.sbuf_tensor("tchain", [1, 8], F32))
        rtempb = _es.enter_context(nc.sbuf_tensor("rtempb", [128, 1], F32))
        rs12 = _es.enter_context(nc.sbuf_tensor("rs12", [128, 2], F32))
        accw = _es.enter_context(nc.sbuf_tensor("accw", [128, 2], F32))
        AR = _es.enter_context(nc.sbuf_tensor("AR", [128, ARN], F16))
        ps_a = _es.enter_context(nc.psum_tensor("ps_a", [128, 512], F32))
        ps_b = _es.enter_context(nc.psum_tensor("ps_b", [128, 512], F32))
        ps_c = _es.enter_context(nc.psum_tensor("ps_c", [128, 512], F32))
        ps_d = _es.enter_context(nc.psum_tensor("ps_d", [128, 512], F32))
        ps_av1 = _es.enter_context(nc.psum_tensor("ps_av1", [65, 512], F32))
        ps_av2 = _es.enter_context(nc.psum_tensor("ps_av2", [65, 512], F32))
        ps_t1 = _es.enter_context(nc.psum_tensor("ps_t1", [128, 128], F16))
        ps_t2 = _es.enter_context(nc.psum_tensor("ps_t2", [128, 128], F16))
        s_ld = _es.enter_context(nc.semaphore("s_ld"))
        s_g = _es.enter_context(nc.semaphore("s_g"))
        s_pe = _es.enter_context(nc.semaphore("s_pe"))
        s_act = _es.enter_context(nc.semaphore("s_act"))
        s_dve = _es.enter_context(nc.semaphore("s_dve"))
        s_gp = _es.enter_context(nc.semaphore("s_gp"))
        s_w1 = [_es.enter_context(nc.semaphore(f"s_w1{i}")) for i in range(2)]
        s_w2s = _es.enter_context(nc.semaphore("s_w2s"))
        s_tw = [_es.enter_context(nc.semaphore(f"s_tw{i}")) for i in range(2)]
        block = _es.enter_context(nc.Block())
        P = Prog()
        P.bind({"ld": s_ld, "g": s_g, "pe": s_pe, "act": s_act,
                "dve": s_dve, "gp": s_gp, "w10": s_w1[0], "w11": s_w1[1],
                "w2s": s_w2s, "tw0": s_tw[0], "tw1": s_tw[1]})

        def v2(off, n, p0=0, p1=128):
            assert off + n <= ARN
            return AR[p0:p1, off:off + n]

        def v3(off, a, b, p0=0, p1=128):
            assert off + a * b <= ARN
            return AR[p0:p1, off:off + a * b].rearrange(
                "p (a b) -> p a b", a=a, b=b)

        def vf32(off, n, p0=0, p1=128):
            assert off % 2 == 0 and off + 2 * n <= ARN
            return AR[p0:p1, off:off + 2 * n].bitcast(F32)

        def vf32_3(off, a, b):
            assert off % 2 == 0 and off + 2 * a * b <= ARN
            return AR[:, off:off + 2 * a * b].bitcast(F32).rearrange(
                "p (a b) -> p a b", a=a, b=b)

        o4 = 65536
        xT_LN = v3(0, 8, 4096)
        x2 = vf32_3(0, 16, 1024)
        xT_LNq = v3(32768, 8, 2048)
        owo_fm = v3(32768, 8, 2048)
        O_all = v3(49152, 8, 2048)
        hg = v3(32768, 32, 1024)
        tw32 = [vf32(32768, 8192), vf32(49152, 8192)]

        kbuf = v2(o4 + 0, 4096)
        ktmp = v2(o4 + 4096, 4096)
        ktm = v3(o4 + 8192, 32, 128)
        wsl = v3(o4 + 8192, 24, 128)
        Et = [v2(o4 + 8192 + i * 512, 512) for i in range(4)]
        wb = [v2(o4 + 10240 + i * 512, 512) for i in range(4)]
        mbuf = v2(o4 + 27920, 512)
        drow = vf32(o4 + 28944, 512, 0, 1)
        drow16 = v2(o4 + 29968, 512, 0, 1)
        otmp = v2(o4 + 30480, 512, 0, 65)
        vtm = v3(o4 + 12288, 32, 132)
        qbuf = v2(o4 + 16512, 2048)
        qtmp = v2(o4 + 18560, 2048)
        Tb = v3(o4 + 20608, 2, 2048)
        rk_c = v2(o4 + 24704, 512, 0, 2)
        rq_c = v2(o4 + 25216, 512, 0, 2)
        lnb = vf32(o4 + 25728, 512, 0, 2)
        m2h = [v2(o4 + 26752, 64, 0, 64), v2(o4 + 26816, 64, 64, 128)]
        ksum16 = v2(o4 + 26880, 1)
        Bbuf_h = [v2(o4 + 26896, 512, 0, 64), v2(o4 + 26896, 512, 64, 128)]
        rb65 = v2(o4 + 27408, 512, 0, 65)
        strow = vf32(o4 + 27408, 512)
        sigv = vf32(o4 + 4096, 512)
        musq = vf32(o4 + 8192, 512)
        Trow16 = v2(o4 + 18560, 512)
        avraw = [vf32(o4 + 33568, 512, 0, 65), vf32(o4 + 34592, 512, 0, 65)]

        ep_x32 = [vf32(o4 + 0, 1024), vf32(o4 + 2048, 1024)]
        ep_xf = v2(o4 + 4096, 1024)
        ep_fm = v3(o4 + 5120, 8, 128)
        ep_gl = v2(o4 + 6144, 1024)
        ep_gate = v2(o4 + 7168, 1024)
        ep_ow = v2(o4 + 8192, 1024)
        ep_t1 = v2(o4 + 9216, 1024)
        ep_t2 = vf32(o4 + 10240, 1024)
        wg_sb = v3(o4 + 12288, 8, 1024)
        wo_sb = v3(o4 + 20480, 8, 1024)
        sqt = vf32(o4 + 28672, 1024)
        xin = [vf32(o4 + 12288, 1024), vf32(o4 + 14336, 1024)]
        xf16 = [v2(o4 + 16384, 1024), v2(o4 + 17408, 1024)]
        xln16 = v2(o4 + 18432, 1024)
        f_x2l = v3(o4 + 0, 8, 1024)
        f_lnout = v2(o4 + 8448, 1024)
        f_fmo = v2(o4 + 9472, 1024)
        f_t32 = vf32(o4 + 10496, 128)
        w1stage = [v3(o4 + 31520, 8, 128), v3(o4 + 32544, 8, 128)]
        w2stage = v3(o4 + 0, 32, 128)
        t_coll = v3(o4 + 4096, 32, 128)
        y_stage = vf32_3(o4 + 12288, 8, 1024)
        t_junk = v2(o4 + 12288, 4096)
        t_tr = v2(o4 + 0, 4096)
        t_ftm = v2(o4 + 4096, 128)


        def ln_tile(x32ap, out16ap, wait_extra=()):
            s1 = small[:, 0:1]
            s2 = small[:, 1:2]
            mu = small[:, 2:3]
            m2s = small[:, 3:4]
            rstd = small[:, 4:5]
            w0 = list(wait_extra)
            P.on("vector", lambda v: v.reduce_sum(s1, x32ap, axis=AX),
                 wait=w0 + P.now("act"))
            P.on("scalar", lambda sc: sc.activation(sqt[:, :], x32ap,
                                                    Act.Square),
                 wait=w0 + P.now("dve"), inc="act")
            P.on("vector", lambda v: v.reduce_sum(s2, sqt[:, :], axis=AX),
                 wait=P.now("act"))
            P.on("vector", lambda v: v.tensor_scalar(
                mu, s1, 1.0 / D, None, Alu.mult))
            P.on("vector", lambda v: v.tensor_scalar(
                m2s, s2, 1.0 / D, None, Alu.mult))
            P.on("vector", lambda v: v.tensor_tensor(
                rstd, mu, mu, Alu.mult))
            P.on("vector", lambda v: v.tensor_copy(
                small[:, 15:16], small[:, 14:15]))
            P.on("vector", lambda v: v.tensor_tensor(
                m2s, m2s, rstd, Alu.subtract), inc="dve")
            P.on("scalar", lambda sc: sc.activation(
                rstd, m2s, Act.Ln, bias=cst[:, 0:1]), wait=P.now("dve"))
            P.on("scalar", lambda sc: sc.activation(
                rstd, rstd, Act.Exp, scale=cst[:, 3:4]), inc="act")
            P.on("vector", lambda v: v.tensor_scalar(
                out16ap, x32ap, mu, rstd, Alu.subtract, Alu.mult),
                wait=P.now("act"), inc="dve")

        def bias_sl(i, c):
            return bias5[:, i * 8 + c:i * 8 + c + 1]

        def dump(nm, src_ap):
            if debug and nm in dbg:
                P.on("gpsimd", lambda g, nm=nm, src_ap=src_ap: g.dma_start(
                    out=dbg[nm], in_=src_ap),
                    wait=P.now("dve", "act", "pe", "g", "ld"),
                    inc="g", inc_n=16)

        P.on("gpsimd", lambda g: g.memset(ident[:], 1.0))
        P.on("gpsimd", lambda g: g.affine_select(
            ident[:], ident[:], pattern=[[-1, 128]],
            compare_op=Alu.is_equal, fill=0.0, base=0,
            channel_multiplier=1), inc="gp")
        P.on("vector", lambda v: v.memset(ones_r[:], 1.0))
        P.on("vector", lambda v: v.memset(ones_c[:], 1.0))
        P.on("vector", lambda v: v.memset(bd2[:], 0.0))
        P.on("vector", lambda v: v.memset(bd2[0:64, 0:1], 1.0))
        P.on("vector", lambda v: v.memset(bd2[64:128, 1:2], 1.0))
        P.on("vector", lambda v: v.memset(ones32[:, :], 1.0))
        P.on("vector", lambda v: v.memset(ones_f[:, :], 1.0))
        P.on("vector", lambda v: v.memset(cst[:, 0:1], LN_EPS))
        P.on("vector", lambda v: v.memset(cst[:, 1:2], 1e-24))
        P.on("vector", lambda v: v.memset(cst[:, 2:3], 1e-12))
        P.on("vector", lambda v: v.memset(cst[:, 3:4], -0.5))
        P.on("vector", lambda v: v.memset(cst[:, 4:5], 0.5), inc="dve")
        P.on("tensor", lambda te: te.transpose(
            ps_t1[0:2, :], bd2[:, :], ident[:]),
            wait=P.now("dve") + P.now("gp"), inc="pe")
        P.on("vector", lambda v: v.tensor_copy(bd2T[:, :], ps_t1[0:2, :]),
             wait=P.now("pe"), inc="dve")
        for i, nm in enumerate(("bq", "bk", "bv", "bo", "bg", "b2")):
            for c in range(8):
                P.on("sync", lambda s, nm=nm, i=i, c=c: s.dma_start(
                    out=bias5[:, i * 8 + c:i * 8 + c + 1],
                    in_=bias_d[nm][c * 128:(c + 1) * 128].rearrange("(a b) -> a b", b=1)), inc="ld", inc_n=16)
        for c in range(32):
            P.on("sync", lambda s, c=c: s.dma_start(
                out=b1s[:, c:c + 1],
                in_=b1_d[c * 128:(c + 1) * 128].rearrange("(a b) -> a b", b=1)),
                inc="ld", inc_n=16)
        P.on("sync", lambda s: s.dma_start(
            out=wtrow[:], in_=Wt_d[:, 0].rearrange("(b a) -> b a", b=1)),
            inc="ld", inc_n=16)
        P.on("sync", lambda s: s.dma_start(
            out=tchain[0:1, 0:1], in_=bt_d.rearrange("(b a) -> b a", b=1)),
            inc="ld", inc_n=16)

        ntile = S // 128
        for t in range(ntile + Q // 128):
            is_q = t >= ntile
            ti = t - ntile if is_q else t
            src = xq_d if is_q else xkv
            dstT = xT_LNq if is_q else xT_LN
            bi = t % 2
            P.on("sync", lambda s, src=src, ti=ti, bi=bi: s.dma_start(
                out=xin[bi], in_=src[ti * 128:(ti + 1) * 128, :]),
                wait=P.now("dve"), inc="ld", inc_n=16)
            if not is_q:
                P.on("gpsimd", lambda g, ti=ti, bi=bi: g.dma_start(
                    out=xf16[bi], in_=xkv[ti * 128:(ti + 1) * 128, :]),
                    wait=P.now("pe"), inc="g", inc_n=16)
            ln_tile(xin[bi], xln16[:, :],
                    wait_extra=P.now("ld") + P.now("pe"))
            if not is_q:
                gw = P.now("g")
                for half, ps in ((0, ps_c), (1, ps_d)):
                    P.on("tensor", lambda te, bi=bi, half=half, ps=ps, ti=ti:
                         te.matmul(ps[0:1, :], ones_c[:, :],
                                   xf16[bi][:, half * 512:(half + 1) * 512],
                                   start=(ti == 0), stop=(ti == ntile - 1),
                                   skip_group_check=True),
                         wait=gw, inc="pe")
            dw = P.now("dve")
            for c in range(8):
                pst = ps_t1 if c % 2 == 0 else ps_t2
                P.on("tensor", lambda te, c=c, pst=pst: te.transpose(
                    pst[:], xln16[:, c * 128:(c + 1) * 128], ident[:]),
                    wait=dw + (P.now("dve") if c >= 2 else []), inc="pe")
                P.on("vector", lambda v, c=c, ti=ti, pst=pst, dstT=dstT:
                     v.tensor_copy(dstT[:, c, ti * 128:(ti + 1) * 128],
                                   pst[:]), wait=P.now("pe"), inc="dve")
        P.on("vector", lambda v: v.tensor_copy(tmp1[0:1, 0:512], ps_c[0:1, :]),
             wait=P.now("pe"))
        P.on("vector", lambda v: v.tensor_copy(tmp1[0:1, 512:1024],
                                               ps_d[0:1, :]))
        P.on("vector", lambda v: v.tensor_tensor(
            tmp1[:], tmp1[:], wtrow[:], Alu.mult))
        P.on("vector", lambda v: v.reduce_sum(
            tchain[0:1, 1:2], tmp1[:], axis=AX), inc="dve")
        P.on("vector", lambda v: v.tensor_copy(
            small[:, 15:16], small[:, 14:15]))
        P.on("vector", lambda v: v.tensor_scalar(
            tchain[0:1, 2:3], tchain[0:1, 1:2], 1.0 / S, tchain[0:1, 0:1],
            Alu.mult, Alu.add), wait=P.now("ld"))
        P.on("scalar", lambda sc: sc.activation(
            tchain[0:1, 3:4], tchain[0:1, 2:3], Act.Sigmoid),
            wait=P.now("dve"), inc="act")
        P.on("vector", lambda v: v.tensor_scalar(
            tchain[0:1, 4:5], tchain[0:1, 3:4], 0.5, None, Alu.add),
            wait=P.now("act"))
        P.on("vector", lambda v: v.tensor_copy(
            small[:, 15:16], small[:, 14:15]))
        P.on("vector", lambda v: v.reciprocal(
            tchain[0:1, 5:6], tchain[0:1, 4:5]), inc="dve")
        P.on("tensor", lambda te: te.matmul(
            ps_d[:, 0:1], ones32[:, :], tchain[0:1, 5:6],
            start=True, stop=True), wait=P.now("dve"), inc="pe")
        P.on("vector", lambda v: v.tensor_copy(rtempb[:, :], ps_d[:, 0:1]),
             wait=P.now("pe"), inc="dve")

        dump("dbg_xtln", xT_LN[:, 0, :])
        dump("dbg_tchain", tchain[:, :])
        lastc = {}

        def use_ps(key):
            return [w for w in (lastc.get(key),) if w]

        def took(key, sem):
            lastc[key] = (sem, P.cnt(sem))

        pend_norm = []
        avr_done = {}

        def ph1(hp):
            c0 = hp * 128
            def norm_flush(hp, cur_h2):
                h2, qc, ai = pend_norm.pop(0)
                bps = ps_av2 if cur_h2 == 0 else ps_av1
                P.on("vector", lambda v, ai=ai: v.reciprocal(
                    drow[:, :], avraw[ai][0:1, :]),
                    wait=[("dve", avr_done[(h2, qc)])])
                P.on("vector", lambda v: v.tensor_copy(
                    drow16[:, :], drow[:, :]),
                    wait=use_ps("otmp"), inc="dve")
                P.on("tensor", lambda te, bps=bps: te.matmul(
                    bps[0:65, :], ones_r[:, 0:65], drow16[:, :],
                    start=True, stop=True),
                    wait=P.now("dve") + use_ps(("avb", id(bps))), inc="pe")
                P.on("vector", lambda v, bps=bps: v.tensor_copy(
                    rb65[:, :], bps[0:65, :]), wait=P.now("pe"),
                    inc="dve")
                took(("avb", id(bps)), "dve")
                P.on("vector", lambda v, ai=ai: v.tensor_tensor(
                    otmp[:, :], avraw[ai][:, :], rb65[:, :], Alu.mult),
                    inc="dve")
                took(("avr", ai), "dve")
                P.on("gpsimd", lambda g, hp=hp, h2=h2, qc=qc:
                     g.dma_start(
                         out=O_all[h2 * 64:(h2 + 1) * 64, hp,
                                   qc * 512:(qc + 1) * 512],
                         in_=otmp[1:65, :]),
                     wait=P.now("dve"), inc="g", inc_n=16)
                took("otmp", "g")

            if hp == 0:
                P.on("vector", lambda v: v.memset(vtm[:, :, 0:1], 1.0),
                     wait=P.now("ld", "g", "pe"))
                P.on("vector", lambda v: v.memset(vtm[:, :, 66:67], 1.0),
                     inc="dve")
            for wi, wsrc in ((0, Wq), (1, Wk), (2, Wv)):
                P.on("gpsimd", lambda g, wi=wi, wsrc=wsrc, c0=c0:
                     g.dma_start(
                         out=wsl[:, wi * 8:(wi + 1) * 8, :],
                         in_=wsrc[:, c0:c0 + 128].rearrange(
                             "(dc p) f -> p dc f", p=128)),
                     wait=use_ps("wsl") if wi == 0 else [],
                     inc="g", inc_n=16)
            gw = P.now("g")

            def proj(kind, nsl, wbase, dst_fm, bias_i, xsrc, nchunk):
                pend = []

                def norm_tail(kc):
                    sq_src = ktmp if kind == "k" else qtmp
                    P.on("tensor", lambda te, kc=kc, sq_src=sq_src:
                         te.matmul(
                             ps_c[0:2, :], bd2[:, :],
                             sq_src[:, kc * 512:(kc + 1) * 512],
                             start=True, stop=True),
                         wait=[("act", sq_done[kc])] + use_ps("ps_c2"),
                         inc="pe")
                    P.on("scalar", lambda sc: sc.activation(
                        lnb[:, :], ps_c[0:2, :], Act.Ln,
                        bias=cst[0:2, 1:2]),
                        wait=P.now("pe") + use_ps("lnb"), inc="act")
                    took("ps_c2", "act")
                    rr = rk_c if kind == "k" else rq_c
                    P.on("scalar", lambda sc, rr=rr: sc.activation(
                        rr[:, :], lnb[:, :], Act.Exp,
                        scale=cst[0:2, 3:4]), inc="act")
                    took("lnb", "act")
                    if kind == "q":
                        P.on("vector", lambda v: v.tensor_scalar(
                            rq_c[:, :], rq_c[:, :], rtempb[0:2, 0:1],
                            None, Alu.mult), wait=P.now("act"),
                            inc="dve")
                    P.on("tensor", lambda te, rr=rr: te.matmul(
                        ps_d[:, :], bd2T[:, :], rr[:, :],
                        start=True, stop=True),
                        wait=(P.now("dve") if kind == "q"
                              else P.now("act")) + use_ps("ps_dn"),
                        inc="pe")
                    P.on("vector", lambda v, kc=kc: v.tensor_tensor(
                        dst_fm[:, kc * 512:(kc + 1) * 512],
                        dst_fm[:, kc * 512:(kc + 1) * 512],
                        ps_d[:, :], Alu.mult),
                        wait=P.now("pe"), inc="dve")
                    took("ps_dn", "dve")

                sq_done = {}
                for kc in range(nchunk):
                    ps = ps_a if kc % 2 == 0 else ps_b
                    for dc in range(8):
                        P.on("tensor", lambda te, kc=kc, dc=dc, ps=ps:
                             te.matmul(
                                 ps[:], wsl[:, wbase + dc, :],
                                 xsrc[:, dc, kc * 512:(kc + 1) * 512],
                                 start=(dc == 0), stop=(dc == 7)),
                             wait=((gw if kind == "k" else []) +
                                   use_ps(id(ps)) if dc == 0 else []),
                             inc="pe")
                    pw = P.now("pe")
                    P.on("scalar", lambda sc, kc=kc, ps=ps: sc.activation(
                        dst_fm[:, kc * 512:(kc + 1) * 512], ps[:],
                        Act.Identity, bias=bias_sl(bias_i, hp)),
                        wait=pw, inc="act")
                    if kind != "v":
                        sq_dst = ktmp if kind == "k" else qtmp
                        P.on("scalar", lambda sc, kc=kc, ps=ps, sq_dst=sq_dst:
                             sc.activation(
                                 sq_dst[:, kc * 512:(kc + 1) * 512], ps[:],
                                 Act.Square, bias=bias_sl(bias_i, hp)),
                             inc="act")
                        sq_done[kc] = P.cnt("act")
                    took(id(ps), "act")
                    if kind != "v":
                        pend.append(kc)
                        if len(pend) >= 2:
                            norm_tail(pend.pop(0))
                for kc in pend:
                    norm_tail(kc)

            proj("k", None, 8, kbuf, 1, xT_LN, 8)
            proj("q", None, 0, qbuf, 0, xT_LNq, 4)
            proj("v", None, 16, ktmp, 2, xT_LN, 8)
            took("wsl", "pe")

            for src_buf, kind in ((ktmp, "v"), (kbuf, "kn")):
                for kt in range(32):
                    pst = ps_t1 if kt % 2 == 0 else ps_t2
                    P.on("tensor", lambda te, kt=kt, pst=pst, src_buf=src_buf:
                         te.transpose(
                             pst[:], src_buf[:, kt * 128:(kt + 1) * 128],
                             ident[:]),
                         wait=(P.now("act" if kind == "v" else "dve")
                               if kt == 0 else []) + use_ps(id(pst)),
                         inc="pe")
                    pw = P.now("pe")
                    if kind == "v":
                        P.on("vector", lambda v, kt=kt, pst=pst:
                             v.tensor_copy(vtm[:, kt, 1:65], pst[:, 0:64]),
                             wait=pw)
                        P.on("vector", lambda v, kt=kt, pst=pst:
                             v.tensor_copy(vtm[:, kt, 67:131],
                                           pst[:, 64:128]), inc="dve")
                    else:
                        P.on("vector", lambda v, kt=kt, pst=pst:
                             v.tensor_copy(ktm[:, kt, :], pst[:]),
                             wait=pw, inc="dve")
                    took(id(pst), "dve")

            for h2, psx in ((0, ps_c), (1, ps_d)):
                for kt in range(32):
                    P.on("tensor", lambda te, h2=h2, kt=kt, psx=psx:
                         te.matmul(psx[h2 * 64:(h2 + 1) * 64, 0:64],
                                   ktm[:, kt, h2 * 64:(h2 + 1) * 64],
                                   ktm[:, kt, h2 * 64:(h2 + 1) * 64],
                                   start=(kt == 0), stop=(kt == 31),
                                   tile_position=(0, h2 * 64)),
                         wait=((P.now("dve") + use_ps("ps_c2") +
                                use_ps("ps_dn")) if kt == 0 and h2 == 0
                               else []),
                         inc="pe")
            for kt in range(32):
                P.on("tensor", lambda te, kt=kt: te.matmul(
                    ps_a[:, 0:1], ktm[:, kt, :], ones_c[:, :],
                    start=(kt == 0), stop=(kt == 31)),
                    wait=(use_ps(id(ps_a)) if kt == 0 else []), inc="pe")
            pw = P.now("pe")
            P.on("vector", lambda v: v.tensor_copy(m2h[0][:, :],
                                                   ps_c[0:64, 0:64]), wait=pw)
            P.on("vector", lambda v: v.tensor_copy(m2h[1][:, :],
                                                   ps_d[64:128, 0:64]))
            P.on("vector", lambda v: v.tensor_copy(ksum16[:, :],
                                                   ps_a[:, 0:1]), inc="dve")
            took(id(ps_a), "dve")
            took("ps_c2", "dve")
            took("ps_dn", "dve")

            for h2 in range(2):
                r0 = h2 * 64
                psv = ps_av1 if h2 == 0 else ps_av2
                for qc in range(4):
                    qsl = qbuf[r0:r0 + 64, qc * 512:(qc + 1) * 512]
                    p4 = 32 * qc
                    P.on("tensor", lambda te, h2=h2, r0=r0, qsl=qsl:
                         te.matmul(
                             ps_a[r0:r0 + 64, :], m2h[h2][:, :], qsl,
                             start=True, stop=True, tile_position=(r0, r0)),
                         wait=P.now("dve") + use_ps(id(ps_a)), inc="pe")
                    P.on("vector", lambda v, h2=h2, r0=r0, qsl=qsl:
                         v.tensor_tensor(
                             Bbuf_h[h2][:, :], qsl, ps_a[r0:r0 + 64, :],
                             Alu.mult), wait=P.now("pe"), inc="dve")
                    took(id(ps_a), "dve")
                    P.on("tensor", lambda te, r0=r0, p4=p4, qsl=qsl:
                         te.matmul(
                             ps_b[p4:p4 + 1, :], ksum16[r0:r0 + 64, :], qsl,
                             start=True, stop=True, tile_position=(r0, p4)),
                         wait=P.now("dve") +
                         (use_ps(id(ps_b)) if qc == 0 else []), inc="pe")
                    P.on("tensor", lambda te, h2=h2, r0=r0, p4=p4:
                         te.matmul(
                             ps_c[p4:p4 + 1, :], ones_c[r0:r0 + 64, :],
                             Bbuf_h[h2][:, :],
                             start=True, stop=True, tile_position=(r0, p4)),
                         wait=(use_ps(id(ps_c)) if qc == 0 else []),
                         inc="pe")
                pw = P.now("pe")
                P.on("vector", lambda v: v.tensor_copy(
                    strow[:, :], ps_b[:, :]), wait=pw + use_ps("otmp"))
                took(id(ps_b), "dve")
                P.on("vector", lambda v: v.tensor_copy(
                    sigv[:, :], ps_c[:, :]), inc="dve")
                took(id(ps_c), "dve")
                P.on("vector", lambda v: v.tensor_scalar(
                    strow[:, :], strow[:, :], 1.0 / S, None, Alu.mult),
                    wait=P.now("dve"))
                P.on("vector", lambda v: v.tensor_scalar(
                    sigv[:, :], sigv[:, :], 1.0 / S, None, Alu.mult))
                P.on("vector", lambda v: v.tensor_tensor(
                    musq[:, :], strow[:, :], strow[:, :], Alu.mult))
                P.on("vector", lambda v: v.tensor_copy(
                    small[:, 15:16], small[:, 14:15]))
                P.on("vector", lambda v: v.tensor_tensor(
                    sigv[:, :], sigv[:, :], musq[:, :], Alu.subtract),
                    inc="dve")
                P.on("scalar", lambda sc: sc.activation(
                    sigv[:, :], sigv[:, :], Act.Ln, bias=cst[:, 2:3]),
                    wait=P.now("dve"))
                P.on("scalar", lambda sc: sc.activation(
                    sigv[:, :], sigv[:, :], Act.Exp,
                    scale=cst[:, 4:5]), inc="act")
                P.on("vector", lambda v: v.tensor_scalar(
                    sigv[:, :], sigv[:, :], ZINIT, None, Alu.mult),
                    wait=P.now("act"))
                P.on("vector", lambda v: v.tensor_copy(
                    small[:, 15:16], small[:, 14:15]))
                P.on("vector", lambda v: v.tensor_tensor(
                    strow[:, :], strow[:, :], sigv[:, :], Alu.add),
                    inc="dve")
                P.on("scalar", lambda sc: sc.activation(
                    strow[:, :], strow[:, :], Act.Exp),
                    wait=P.now("dve"), inc="act")
                P.on("vector", lambda v: v.tensor_copy(
                    Trow16[:, :], strow[:, :]), wait=P.now("act"),
                    inc="dve")
                for qc in range(4):
                    P.on("tensor", lambda te, qc=qc: te.matmul(
                        ps_d[:, :], ones_f[32 * qc:32 * qc + 1, :],
                        Trow16[32 * qc:32 * qc + 1, :],
                        start=True, stop=True, tile_position=(32 * qc, 0)),
                        wait=(P.now("dve") if qc == 0 else []) +
                        use_ps(id(ps_d)), inc="pe")
                    P.on("vector", lambda v, h2=h2, qc=qc: v.tensor_copy(
                        Tb[:, h2, qc * 512:(qc + 1) * 512], ps_d[:, :]),
                        wait=P.now("pe"), inc="dve")
                    took(id(ps_d), "dve")

                rings = (ps_a, ps_b, ps_c, ps_d)

                def emit_mm(kt, qc, qsl):
                    pss = rings[kt % 4]
                    P.on("tensor", lambda te, kt=kt, pss=pss, qsl=qsl,
                         r0=r0: te.matmul(
                             pss[:, :],
                             kbuf[r0:r0 + 64, kt * 128:(kt + 1) * 128],
                             qsl, start=True, stop=True,
                             tile_position=(r0, 0)),
                         wait=(P.now("dve") if kt == 0 and qc == 0
                               else []) + use_ps(id(pss)), inc="pe")

                def emit_maskw(kt, qc):
                    pss = rings[kt % 4]
                    ei = kt % 4
                    P.on("scalar", lambda sc, pss=pss, ei=ei:
                         sc.activation(Et[ei][:, :], pss[:, :], Act.Exp),
                         wait=[("pe", pe_after_mm[kt])] +
                         use_ps(("et", ei)), inc="act")
                    took(id(pss), "act")
                    P.on("vector", lambda v, ei=ei, h2=h2, qc=qc:
                         v.tensor_tensor(
                             mbuf[:, :], Et[ei][:, :],
                             Tb[:, h2, qc * 512:(qc + 1) * 512],
                             Alu.is_ge),
                         wait=P.now("act") + use_ps(("wb", ei)))
                    P.on("vector", lambda v, ei=ei: v.tensor_tensor(
                        wb[ei][:, :], mbuf[:, :], Et[ei][:, :],
                        Alu.mult), inc="dve")
                    took(("et", ei), "dve")
                    dve_after_w[kt] = P.cnt("dve")

                def emit_av(kt):
                    ei = kt % 4
                    st_sl = (vtm[:, kt, 0:65] if h2 == 0
                             else vtm[:, kt, 66:131])
                    P.on("tensor", lambda te, kt=kt, psv=psv,
                         st_sl=st_sl: te.matmul(
                             psv[:, :], st_sl, wb[ei][:, :],
                             start=(kt == 0), stop=(kt == 31)),
                         wait=[("dve", dve_after_w[kt])] +
                         (use_ps(("avb", id(psv))) if kt == 0 else []),
                         inc="pe")
                    took(("wb", ei), "pe")

                for qc in range(4):
                    qsl = qbuf[r0:r0 + 64, qc * 512:(qc + 1) * 512]
                    pe_after_mm = {}
                    dve_after_w = {}
                    for kt in range(4):
                        emit_mm(kt, qc, qsl)
                        pe_after_mm[kt] = P.cnt("pe")
                    for kt in range(32):
                        emit_maskw(kt, qc)
                        if kt > 0:
                            emit_av(kt - 1)
                        if kt + 4 < 32:
                            emit_mm(kt + 4, qc, qsl)
                            pe_after_mm[kt + 4] = P.cnt("pe")
                    emit_av(31)
                    ai = (h2 * 4 + qc) % 2
                    P.on("vector", lambda v, psv=psv, ai=ai: v.tensor_copy(
                        avraw[ai][:, :], psv[0:65, :]),
                        wait=P.now("pe") + use_ps(("avr", ai)), inc="dve")
                    avr_done[(h2, qc)] = P.cnt("dve")
                    pend_norm.append((h2, qc, ai))
                    norm_flush(hp, h2)


        def ph_wo():
            for dc in range(8):
                P.on("gpsimd", lambda g, dc=dc: g.dma_start(
                    out=wo_sb[:, dc, :], in_=Wo_d[dc * 128:(dc + 1) * 128, :]),
                    wait=P.now("dve"), inc="g", inc_n=16)
            gw = P.now("g")
            for dc in range(8):
                for qc in range(4):
                    ps = ps_a if (dc * 4 + qc) % 2 == 0 else ps_b
                    for hp in range(NHP):
                        P.on("tensor", lambda te, dc=dc, qc=qc, hp=hp, ps=ps:
                             te.matmul(
                                 ps[:], wo_sb[:, hp, dc * 128:(dc + 1) * 128],
                                 O_all[:, hp, qc * 512:(qc + 1) * 512],
                                 start=(hp == 0), stop=(hp == 7)),
                             wait=(gw if dc == 0 and qc == 0 and hp == 0
                                   else (P.now("act") if hp == 0 else [])),
                             inc="pe")
                    P.on("scalar", lambda sc, dc=dc, qc=qc, ps=ps:
                         sc.activation(
                             owo_fm[:, dc, qc * 512:(qc + 1) * 512], ps[:],
                             Act.Identity, bias=bias_sl(3, dc)),
                         wait=P.now("pe"), inc="act")

        def ph_epi():
            for dc in range(8):
                P.on("gpsimd", lambda g, dc=dc: g.dma_start(
                    out=wg_sb[:, dc, :], in_=Wg_d[dc * 128:(dc + 1) * 128, :]),
                    wait=(P.now("dve") if dc == 0 else []), inc="g",
                    inc_n=16)
            gw = P.now("g")
            for t in range(Q // 128):
                bi = t % 2
                P.on("sync", lambda s, t=t, bi=bi: s.dma_start(
                    out=ep_x32[bi], in_=xq_d[t * 128:(t + 1) * 128, :]),
                    wait=P.now("dve"), inc="ld", inc_n=16)
                ln_tile(ep_x32[bi], ep_xf[:, :],
                        wait_extra=P.now("ld") + (gw if t == 0 else []))
                dw = P.now("dve")
                for c in range(8):
                    pst = ps_t1 if c % 2 == 0 else ps_t2
                    P.on("tensor", lambda te, c=c, pst=pst: te.transpose(
                        pst[:], ep_xf[:, c * 128:(c + 1) * 128], ident[:]),
                        wait=dw + (P.now("dve") if c >= 2 else []), inc="pe")
                    P.on("vector", lambda v, c=c, pst=pst: v.tensor_copy(
                        ep_fm[:, c, :], pst[:]), wait=P.now("pe"),
                        inc="dve")
                dw = P.now("dve")
                for dc in range(8):
                    ps = ps_a if dc % 2 == 0 else ps_b
                    for di in range(8):
                        P.on("tensor", lambda te, dc=dc, di=di, ps=ps:
                             te.matmul(
                                 ps[:, 0:128],
                                 wg_sb[:, di, dc * 128:(dc + 1) * 128],
                                 ep_fm[:, di, :],
                                 start=(di == 0), stop=(di == 7)),
                             wait=((dw + P.now("act")) if di == 0 else []),
                             inc="pe")
                    P.on("scalar", lambda sc, dc=dc, ps=ps: sc.activation(
                        ep_t1[:, 0:128], ps[:, 0:128], Act.Identity,
                        bias=bias_sl(4, dc)), wait=P.now("pe"), inc="act")
                    pst = ps_t1 if dc % 2 == 0 else ps_t2
                    P.on("tensor", lambda te, pst=pst: te.transpose(
                        pst[:], ep_t1[:, 0:128], ident[:]),
                        wait=P.now("act"), inc="pe")
                    P.on("vector", lambda v, dc=dc, pst=pst: v.tensor_copy(
                        ep_gl[:, dc * 128:(dc + 1) * 128], pst[:]),
                        wait=P.now("pe"), inc="dve")
                P.on("scalar", lambda sc: sc.activation(
                    ep_gate[:, :], ep_gl[:, :], Act.Sigmoid),
                    wait=P.now("dve"), inc="act")
                for c in range(8):
                    pst = ps_t1 if c % 2 == 0 else ps_t2
                    P.on("tensor", lambda te, c=c, t=t, pst=pst: te.transpose(
                        pst[:], owo_fm[:, c, t * 128:(t + 1) * 128],
                        ident[:]), wait=P.now("dve"), inc="pe")
                    P.on("vector", lambda v, c=c, pst=pst: v.tensor_copy(
                        ep_ow[:, c * 128:(c + 1) * 128], pst[:]),
                        wait=P.now("pe"), inc="dve")
                P.on("vector", lambda v: v.tensor_tensor(
                    ep_t1[:, :], ep_ow[:, :], ep_xf[:, :], Alu.subtract),
                    wait=P.now("act"))
                P.on("vector", lambda v: v.tensor_tensor(
                    ep_t1[:, :], ep_gate[:, :], ep_t1[:, :], Alu.mult))
                P.on("vector", lambda v: v.tensor_tensor(
                    ep_t1[:, :], ep_t1[:, :], ep_xf[:, :], Alu.add))
                P.on("vector", lambda v: v.tensor_copy(
                    ep_t2[:, :], ep_t1[:, :]))
                P.on("vector", lambda v, t=t, bi=bi: v.tensor_tensor(
                    x2[:, t, :], ep_x32[bi], ep_t2[:, :], Alu.add),
                    inc="dve")

        def ph_tern():
            for wi, (wsrc, nchunk, width) in enumerate(
                    ((W1_d, 32, 1024), (W2_d, 8, 4096))):
                P.on("vector", lambda v, wi=wi: v.memset(
                    accw[:, wi:wi + 1], 0.0), inc="dve")
                for ch in range(nchunk):
                    bi = ch % 2
                    P.on("sync", lambda s, wsrc=wsrc, ch=ch, bi=bi,
                         width=width: s.dma_start(
                             out=tw32[bi][:, 0:width],
                             in_=wsrc[ch * 128:(ch + 1) * 128, :]),
                         wait=use_ps(("tw", bi)), inc=f"tw{bi}", inc_n=16)
                    P.on("scalar", lambda sc, bi=bi, width=width:
                         sc.activation(t_junk[:, 0:width],
                                       tw32[bi][:, 0:width], Act.Abs,
                                       accum_out=small[:, 6:7]),
                         wait=P.now(f"tw{bi}") + P.now("dve"), inc="act")
                    took(("tw", bi), "act")
                    P.on("vector", lambda v: v.tensor_copy(
                        small[:, 15:16], small[:, 14:15]))
                    P.on("vector", lambda v, wi=wi: v.tensor_tensor(
                        accw[:, wi:wi + 1], accw[:, wi:wi + 1],
                        small[:, 6:7], Alu.add), wait=P.now("act"),
                        inc="dve")
                P.on("gpsimd", lambda g, wi=wi: g.dma_start(
                    out=scr_h[wi * 128:(wi + 1) * 128].rearrange(
                        "(a b) -> a b", a=128),
                    in_=accw[:, wi:wi + 1]),
                    wait=P.now("dve"), inc="g", inc_n=16)
                P.on("gpsimd", lambda g, wi=wi: g.dma_start(
                    out=tmp1[0:1, wi * 128:(wi + 1) * 128],
                    in_=scr_h[wi * 128:(wi + 1) * 128].rearrange(
                        "(a b) -> a b", a=1)),
                    wait=P.now("g"), inc="g", inc_n=16)
                P.on("vector", lambda v, wi=wi: v.reduce_sum(
                    tchain[0:1, 6 + wi:7 + wi],
                    tmp1[0:1, wi * 128:(wi + 1) * 128], axis=AX),
                    wait=P.now("g"))
                P.on("vector", lambda v: v.tensor_copy(
                    small[:, 15:16], small[:, 14:15]))
                P.on("vector", lambda v, wi=wi: v.tensor_scalar(
                    tchain[0:1, 6 + wi:7 + wi], tchain[0:1, 6 + wi:7 + wi],
                    1.0 / (4096.0 * 1024.0), 1e-8, Alu.mult, Alu.max))
                P.on("vector", lambda v: v.tensor_copy(
                    small[:, 15:16], small[:, 14:15]))
                P.on("vector", lambda v, wi=wi: v.reciprocal(
                    tchain[0:1, 6 + wi:7 + wi], tchain[0:1, 6 + wi:7 + wi]),
                    inc="dve")
                P.on("tensor", lambda te, wi=wi: te.matmul(
                    ps_d[:, 0:1], ones32[:, :], tchain[0:1, 6 + wi:7 + wi],
                    start=True, stop=True), wait=P.now("dve"), inc="pe")
                P.on("vector", lambda v, wi=wi: v.tensor_copy(
                    rs12[:, wi:wi + 1], ps_d[:, 0:1]),
                    wait=P.now("pe"), inc="dve")
            gw = P.now("g")
            for wi, (wsrc, dsth, nchunk, width) in enumerate(
                    ((W1_d, w1tT_h, 32, 1024), (W2_d, w2tT_h, 8, 4096))):
                for ch in range(nchunk):
                    bi = ch % 2
                    P.on("sync", lambda s, wsrc=wsrc, ch=ch, bi=bi,
                         width=width: s.dma_start(
                             out=tw32[bi][:, 0:width],
                             in_=wsrc[ch * 128:(ch + 1) * 128, :]),
                         wait=(gw if ch < 2 and wi == 0 else []) +
                         use_ps(("tw", bi)), inc=f"tw{bi}", inc_n=16)
                    lw = P.now(f"tw{bi}")
                    P.on("vector", lambda v, bi=bi, wi=wi, width=width:
                         v.tensor_scalar(
                             tw32[bi][:, 0:width], tw32[bi][:, 0:width],
                             rs12[:, wi:wi + 1], MAGIC, Alu.mult, Alu.add),
                         wait=lw)
                    P.on("vector", lambda v, bi=bi, width=width:
                         v.tensor_scalar(
                             tw32[bi][:, 0:width], tw32[bi][:, 0:width],
                             MAGIC, -1.0, Alu.subtract, Alu.max))
                    P.on("vector", lambda v, bi=bi, width=width:
                         v.tensor_scalar(
                             t_tr[:, 0:width], tw32[bi][:, 0:width],
                             1.0, None, Alu.min),
                         wait=use_ps("t_tr"), inc="dve")
                    took(("tw", bi), "dve")
                    dw = P.now("dve")
                    for tt in range(width // 128):
                        pst = ps_t1 if tt % 2 == 0 else ps_t2
                        P.on("tensor", lambda te, tt=tt, pst=pst:
                             te.transpose(pst[:],
                                          t_tr[:, tt * 128:(tt + 1) * 128],
                                          ident[:]),
                             wait=(dw if tt < 2 else []) + use_ps(id(pst)),
                             inc="pe")
                        P.on("vector", lambda v, tt=tt, pst=pst:
                             v.tensor_copy(t_coll[:, tt, :], pst[:]),
                             wait=P.now("pe") + (P.now("g") if tt == 0
                                                 else []), inc="dve")
                        took(id(pst), "dve")
                    took("t_tr", "pe")
                    P.on("gpsimd", lambda g, dsth=dsth, ch=ch, width=width:
                         g.dma_start(
                             out=dsth[ch],
                             in_=t_coll[:, 0:width // 128, :]),
                         wait=P.now("dve"), inc="g", inc_n=16)

        def ph_ffn():
            for tb in range(2):
                P.barrier("vector", ["g", "ld"])
                for tt in range(8):
                    t = tb * 8 + tt
                    ln_tile(x2[:, t, :], f_lnout[:, :],
                            wait_extra=P.now("pe"))
                    dw = P.now("dve")
                    for c in range(8):
                        pst = ps_t1 if c % 2 == 0 else ps_t2
                        P.on("tensor", lambda te, c=c, pst=pst: te.transpose(
                            pst[:], f_lnout[:, c * 128:(c + 1) * 128],
                            ident[:]),
                            wait=dw + (P.now("dve") if c >= 2 else []),
                            inc="pe")
                        P.on("vector", lambda v, c=c, tt=tt, pst=pst:
                             v.tensor_copy(
                                 f_x2l[:, c, tt * 128:(tt + 1) * 128],
                                 pst[:]), wait=P.now("pe"), inc="dve")
                if tb == 1:
                    dump("dbg_eg", f_x2l[:, 0, 0:512])
                    dump("dbg_x2b", x2[:, 8:16, :].rearrange("p a b -> p (a b)"))
                    dump("dbg_x2l", f_x2l.rearrange("p a b -> p (a b)"))
                for fc in range(32):
                    p0 = ps_a if fc % 2 == 0 else ps_b
                    p1 = ps_c if fc % 2 == 0 else ps_d
                    sbi = fc % 2
                    P.on("sync", lambda s, fc=fc, sbi=sbi: s.dma_start(
                        out=w1stage[sbi][:, :, :],
                        in_=w1tT_h[fc]),
                        wait=use_ps(("w1s", sbi)),
                        inc=f"w1{sbi}", inc_n=16)
                    lw = P.now(f"w1{sbi}")
                    for di in range(8):
                        for half, psx in ((0, p0), (1, p1)):
                            P.on("tensor", lambda te, di=di, psx=psx,
                                 half=half, sbi=sbi: te.matmul(
                                     psx[:, :], w1stage[sbi][:, di, :],
                                     f_x2l[:, di,
                                           half * 512:(half + 1) * 512],
                                     start=(di == 0), stop=(di == 7)),
                                 wait=(lw if di == 0 and half == 0 else
                                       []) + (use_ps(id(psx))
                                              if di == 0 else []),
                                 inc="pe")
                    if tb == 1 and fc == 1:
                        dump("dbg_w1s", w1stage[sbi].rearrange(
                            "p a b -> p (a b)"))
                    pw = P.now("pe")
                    for half, psx in ((0, p0), (1, p1)):
                        P.on("scalar", lambda sc, fc=fc, half=half, psx=psx:
                             sc.activation(
                                 hg[:, fc, half * 512:(half + 1) * 512],
                                 psx[:, :], Act.Gelu,
                                 bias=b1s[:, fc:fc + 1]),
                             wait=(pw if half == 0 else []), inc="act")
                        took(id(psx), "act")
                    took(("w1s", sbi), "act")
                if tb == 1:
                    dump("dbg_tb", hg[:, 0:4, :].rearrange("p a b -> p (a b)"))
                for dc in range(8):
                    p0 = ps_a if dc % 2 == 0 else ps_b
                    p1 = ps_c if dc % 2 == 0 else ps_d
                    P.on("sync", lambda s, dc=dc: s.dma_start(
                        out=w2stage[:, :, :],
                        in_=w2tT_h[dc]),
                        wait=use_ps("w2st") + (P.now("act", "dve")
                                               if dc == 0 else []),
                        inc="w2s", inc_n=16)
                    lw = P.now("w2s")
                    for fc in range(32):
                        for half, psx in ((0, p0), (1, p1)):
                            P.on("tensor", lambda te, fc=fc, psx=psx,
                                 half=half: te.matmul(
                                     psx[:, :], w2stage[:, fc, :],
                                     hg[:, fc, half * 512:(half + 1) * 512],
                                     start=(fc == 0), stop=(fc == 31)),
                                 wait=(lw + P.now("act", "dve")
                                       if fc == 0 and half == 0 else
                                       (use_ps(id(psx)) if fc == 0
                                        else [])), inc="pe")
                    took("w2st", "pe")
                    pw = P.now("pe")
                    for half, psx in ((0, p0), (1, p1)):
                        P.on("scalar", lambda sc, dc=dc, half=half, psx=psx:
                             sc.activation(
                                 f_fmo[:, half * 512:(half + 1) * 512],
                                 psx[:, :], Act.Identity,
                                 bias=bias_sl(5, dc)),
                             wait=(pw if half == 0 else []), inc="act")
                        took(id(psx), "act")
                    aw = P.now("act")
                    for tt in range(8):
                        pst = ps_t1 if tt % 2 == 0 else ps_t2
                        P.on("tensor", lambda te, tt=tt, pst=pst:
                             te.transpose(
                                 pst[:], f_fmo[:, tt * 128:(tt + 1) * 128],
                                 ident[:]),
                             wait=aw + (P.now("dve") if tt >= 2 else []),
                             inc="pe")
                        t = tb * 8 + tt
                        P.on("vector", lambda v, pst=pst: v.tensor_copy(
                            f_t32[:, :], pst[:]), wait=P.now("pe"))
                        P.on("vector", lambda v, t=t, tt=tt, dc=dc:
                             v.tensor_tensor(
                                 y_stage[:, tt, dc * 128:(dc + 1) * 128],
                                 x2[:, t, dc * 128:(dc + 1) * 128],
                                 f_t32[:, :], Alu.add), inc="dve")
                dw = P.now("dve")
                for tt in range(8):
                    t = tb * 8 + tt
                    P.on("gpsimd", lambda g, t=t, tt=tt: g.dma_start(
                        out=y_d[t * 128:(t + 1) * 128, :],
                        in_=y_stage[:, tt, :]),
                        wait=dw, inc="g", inc_n=16)

        for hp in range(NHP):
            ph1(hp)
        dump("dbg_oall", O_all.rearrange("p a b -> p (a b)"))
        P.barrier("tensor", ["dve", "act", "g"])
        P.barrier("gpsimd", ["dve", "act", "pe"])
        ph_wo()
        dump("dbg_owo", owo_fm.rearrange("p a b -> p (a b)"))
        P.barrier("tensor", ["act"])
        P.barrier("vector", ["act"])
        P.barrier("gpsimd", ["act"])
        ph_epi()
        dump("dbg_x2", x2.rearrange("p a b -> p (a b)"))
        P.barrier("sync", ["dve", "act"])
        P.barrier("gpsimd", ["dve", "act"])
        ph_tern()
        dump("dbg_rs", rs12[:, :])
        P.barrier("sync", ["g", "dve", "pe"])
        P.barrier("scalar", ["g", "dve"])
        P.barrier("tensor", ["g"])
        ph_ffn()
        P.barrier("gpsimd", ["g"])
        P.run(block)

    return nc


_NC_CACHE = {}


def kernel(L0_attn_x, L0_attn_Wq, L0_attn_bq, L0_attn_Wk, L0_attn_bk,
           L0_attn_Wv, L0_attn_bv, L0_attn_Wo, L0_attn_bo, L0_attn_Wt,
           L0_attn_bt, L0_attn_Wg, L0_attn_bg, L0_ffn_W1, L0_ffn_b1,
           L0_ffn_W2, L0_ffn_b2):
    x = np.ascontiguousarray(np.asarray(L0_attn_x, dtype=np.float32))
    if "nc" not in _NC_CACHE:
        _NC_CACHE["nc"] = build_program()
    nc = _NC_CACHE["nc"]
    common = {
        "Wq": np.ascontiguousarray(L0_attn_Wq, np.float32),
        "Wk": np.ascontiguousarray(L0_attn_Wk, np.float32),
        "Wv": np.ascontiguousarray(L0_attn_Wv, np.float32),
        "Wo": np.ascontiguousarray(L0_attn_Wo, np.float32),
        "Wg": np.ascontiguousarray(L0_attn_Wg, np.float32),
        "bq": np.ascontiguousarray(L0_attn_bq, np.float32),
        "bk": np.ascontiguousarray(L0_attn_bk, np.float32),
        "bv": np.ascontiguousarray(L0_attn_bv, np.float32),
        "bo": np.ascontiguousarray(L0_attn_bo, np.float32),
        "bg": np.ascontiguousarray(L0_attn_bg, np.float32),
        "Wt": np.ascontiguousarray(L0_attn_Wt, np.float32),
        "bt": np.ascontiguousarray(np.asarray(L0_attn_bt, np.float32).reshape(1)),
        "W1": np.ascontiguousarray(L0_ffn_W1, np.float32),
        "b1": np.ascontiguousarray(L0_ffn_b1, np.float32),
        "W2": np.ascontiguousarray(L0_ffn_W2, np.float32),
        "b2": np.ascontiguousarray(L0_ffn_b2, np.float32),
    }
    in_maps = []
    for c in range(8):
        b, j = c // 2, c % 2
        m = dict(common)
        m["xkv"] = np.ascontiguousarray(x[b])
        m["xq"] = np.ascontiguousarray(x[b, j * Q:(j + 1) * Q])
        in_maps.append(m)
    res = run_bass_kernel_spmd(nc, in_maps, core_ids=list(range(8)))
    out = np.empty((B, S, D), np.float32)
    for c in range(8):
        b, j = c // 2, c % 2
        out[b, j * Q:(j + 1) * Q] = res.results[c]["y"]
    return out


# version 21
# speedup vs baseline: 2.3441x; 1.0017x over previous
import numpy as np
import concourse.bass as bass
import concourse.mybir as mybir
from concourse.bass_utils import run_bass_kernel_spmd

F32 = mybir.dt.float32
F16 = mybir.dt.float16
Alu = mybir.AluOpType
Act = mybir.ActivationFunctionType
AX = mybir.AxisListType.X

B, S, D = 4, 4096, 1024
Q = 2048
NHP = 8
ZINIT = 0.67448975
MAGIC = 12582912.0
LN_EPS = 1e-5

ENGINES = ("sync", "vector", "scalar", "tensor", "gpsimd")


class Prog:
    def __init__(self):
        self.streams = {e: [] for e in ENGINES}
        self.sems = {}
        self.handles = {}

    def bind(self, mapping):
        for n, h in mapping.items():
            self.handles[n] = h
            self.sems[n] = 0

    def on(self, eng, fn, wait=(), inc=None, inc_n=1):
        wl = []
        for w in wait:
            if isinstance(w, tuple):
                if w[1] > 0:
                    wl.append((self.handles[w[0]], w[1]))
            elif self.sems[w] > 0:
                wl.append((self.handles[w], self.sems[w]))
        it = None
        if inc is not None:
            self.sems[inc] += inc_n
            it = (self.handles[inc], self.sems[inc] - self.sems[inc] + inc_n)
            it = (self.handles[inc], inc_n)
        self.streams[eng].append((wl, fn, it))

    def cnt(self, name):
        return self.sems[name]

    def now(self, *names):
        return [(n, self.sems[n]) for n in names]

    def barrier(self, eng, names):
        wl = [(self.handles[n], self.sems[n]) for n in names
              if self.sems[n] > 0]
        if wl:
            self.streams[eng].append((wl, None, None))

    def run(self, block):
        def mk(ename):
            def body(eng):
                for wl, fn, it in self.streams[ename]:
                    for h, v in wl:
                        eng.wait_ge(h, v)
                    inst = fn(eng) if fn is not None else None
                    if it is not None:
                        inst.then_inc(it[0], it[1])
            return body
        block.sync(mk("sync"))
        block.vector(mk("vector"))
        block.scalar(mk("scalar"))
        block.tensor(mk("tensor"))
        block.gpsimd(mk("gpsimd"))


def build_program(debug=False):
    nc = bass.Bass("TRN2", target_bir_lowering=False, debug=False)

    xkv = nc.dram_tensor("xkv", [S, D], F32, kind="ExternalInput").ap()
    xq_d = nc.dram_tensor("xq", [Q, D], F32, kind="ExternalInput").ap()
    Wq = nc.dram_tensor("Wq", [D, D], F32, kind="ExternalInput").ap()
    Wk = nc.dram_tensor("Wk", [D, D], F32, kind="ExternalInput").ap()
    Wv = nc.dram_tensor("Wv", [D, D], F32, kind="ExternalInput").ap()
    Wo_d = nc.dram_tensor("Wo", [D, D], F32, kind="ExternalInput").ap()
    Wg_d = nc.dram_tensor("Wg", [D, D], F32, kind="ExternalInput").ap()
    bias_d = {}
    for nm in ("bq", "bk", "bv", "bo", "bg", "b2"):
        bias_d[nm] = nc.dram_tensor(nm, [D], F32, kind="ExternalInput").ap()
    b1_d = nc.dram_tensor("b1", [4 * D], F32, kind="ExternalInput").ap()
    Wt_d = nc.dram_tensor("Wt", [D, 1], F32, kind="ExternalInput").ap()
    bt_d = nc.dram_tensor("bt", [1], F32, kind="ExternalInput").ap()
    W1_d = nc.dram_tensor("W1", [4 * D, D], F32, kind="ExternalInput").ap()
    W2_d = nc.dram_tensor("W2", [D, 4 * D], F32, kind="ExternalInput").ap()
    y_d = nc.dram_tensor("y", [Q, D], F32, kind="ExternalOutput").ap()
    scr_h = nc.dram_tensor("scr", [256], F32, kind="Internal").ap()
    dbg = {}
    if debug:
        for nm, shape, dt_ in (
                ("dbg_xtln", [128, 4096], F16), ("dbg_tchain", [1, 8], F32),
                ("dbg_kn", [128, 4096], F16), ("dbg_qn", [128, 2048], F16),
                ("dbg_vtm", [128, 4224], F16), ("dbg_tb", [128, 4096], F16),
                ("dbg_otmp", [65, 512], F16), ("dbg_oall", [128, 16384], F16),
                ("dbg_owo", [128, 16384], F16), ("dbg_x2", [128, 16384], F32),
                ("dbg_m2h", [64, 64], F16), ("dbg_ksum", [128, 1], F16),
                ("dbg_strow", [1, 512], F32), ("dbg_eg", [128, 512], F16),
                ("dbg_rs", [128, 2], F32), ("dbg_x2b", [128, 8192], F32),
                ("dbg_x2l", [128, 8192], F16), ("dbg_w1s", [128, 1024], F16)):
            dbg[nm] = nc.dram_tensor(nm, shape, dt_, kind="ExternalOutput").ap()
    w1tT_h = nc.dram_tensor("w1tT", [32, 128, 8, 128], F16, kind="Internal").ap()
    w2tT_h = nc.dram_tensor("w2tT", [8, 128, 32, 128], F16, kind="Internal").ap()

    ARN = 101248

    import contextlib
    with contextlib.ExitStack() as _es:
        ident = _es.enter_context(nc.sbuf_tensor("ident", [128, 128], F16))
        ones_r = _es.enter_context(nc.sbuf_tensor("ones_r", [1, 128], F16))
        ones_c = _es.enter_context(nc.sbuf_tensor("ones_c", [128, 1], F16))
        bd2 = _es.enter_context(nc.sbuf_tensor("bd2", [128, 2], F16))
        bd2T = _es.enter_context(nc.sbuf_tensor("bd2T", [2, 128], F16))
        bias5 = _es.enter_context(nc.sbuf_tensor("bias5", [128, 48], F32))
        b1s = _es.enter_context(nc.sbuf_tensor("b1s", [128, 32], F32))
        wtrow = _es.enter_context(nc.sbuf_tensor("wtrow", [1, 1024], F32))
        tmp1 = _es.enter_context(nc.sbuf_tensor("tmp1", [1, 1024], F32))
        small = _es.enter_context(nc.sbuf_tensor("small", [128, 16], F32))
        cst = _es.enter_context(nc.sbuf_tensor("cst", [128, 8], F32))
        ones32 = _es.enter_context(nc.sbuf_tensor("ones32", [1, 128], F32))
        ones_f = _es.enter_context(nc.sbuf_tensor("ones_f", [128, 128], F16))
        tchain = _es.enter_context(nc.sbuf_tensor("tchain", [1, 8], F32))
        rtempb = _es.enter_context(nc.sbuf_tensor("rtempb", [128, 1], F32))
        rs12 = _es.enter_context(nc.sbuf_tensor("rs12", [128, 2], F32))
        accw = _es.enter_context(nc.sbuf_tensor("accw", [128, 2], F32))
        AR = _es.enter_context(nc.sbuf_tensor("AR", [128, ARN], F16))
        ps_a = _es.enter_context(nc.psum_tensor("ps_a", [128, 512], F32))
        ps_b = _es.enter_context(nc.psum_tensor("ps_b", [128, 512], F32))
        ps_c = _es.enter_context(nc.psum_tensor("ps_c", [128, 512], F32))
        ps_d = _es.enter_context(nc.psum_tensor("ps_d", [128, 512], F32))
        ps_av1 = _es.enter_context(nc.psum_tensor("ps_av1", [65, 512], F32))
        ps_av2 = _es.enter_context(nc.psum_tensor("ps_av2", [65, 512], F32))
        ps_t1 = _es.enter_context(nc.psum_tensor("ps_t1", [128, 128], F16))
        ps_t2 = _es.enter_context(nc.psum_tensor("ps_t2", [128, 128], F16))
        s_ld = _es.enter_context(nc.semaphore("s_ld"))
        s_g = _es.enter_context(nc.semaphore("s_g"))
        s_pe = _es.enter_context(nc.semaphore("s_pe"))
        s_act = _es.enter_context(nc.semaphore("s_act"))
        s_dve = _es.enter_context(nc.semaphore("s_dve"))
        s_gp = _es.enter_context(nc.semaphore("s_gp"))
        s_w1 = [_es.enter_context(nc.semaphore(f"s_w1{i}")) for i in range(2)]
        s_w2s = _es.enter_context(nc.semaphore("s_w2s"))
        s_tw = [_es.enter_context(nc.semaphore(f"s_tw{i}")) for i in range(2)]
        block = _es.enter_context(nc.Block())
        P = Prog()
        P.bind({"ld": s_ld, "g": s_g, "pe": s_pe, "act": s_act,
                "dve": s_dve, "gp": s_gp, "w10": s_w1[0], "w11": s_w1[1],
                "w2s": s_w2s, "tw0": s_tw[0], "tw1": s_tw[1]})

        def v2(off, n, p0=0, p1=128):
            assert off + n <= ARN
            return AR[p0:p1, off:off + n]

        def v3(off, a, b, p0=0, p1=128):
            assert off + a * b <= ARN
            return AR[p0:p1, off:off + a * b].rearrange(
                "p (a b) -> p a b", a=a, b=b)

        def vf32(off, n, p0=0, p1=128):
            assert off % 2 == 0 and off + 2 * n <= ARN
            return AR[p0:p1, off:off + 2 * n].bitcast(F32)

        def vf32_3(off, a, b):
            assert off % 2 == 0 and off + 2 * a * b <= ARN
            return AR[:, off:off + 2 * a * b].bitcast(F32).rearrange(
                "p (a b) -> p a b", a=a, b=b)

        o4 = 65536
        xT_LN = v3(0, 8, 4096)
        x2 = vf32_3(0, 16, 1024)
        xT_LNq = v3(32768, 8, 2048)
        owo_fm = v3(32768, 8, 2048)
        O_all = v3(49152, 8, 2048)
        hg = v3(32768, 32, 1024)
        tw32 = [vf32(32768, 8192), vf32(49152, 8192)]

        kbuf = v2(o4 + 0, 4096)
        ktmp = v2(o4 + 4096, 4096)
        ktm = v3(o4 + 8192, 32, 128)
        wsl = v3(o4 + 8192, 24, 128)
        Et = [v2(o4 + 8192 + i * 512, 512) for i in range(4)]
        wb = [v2(o4 + 10240 + i * 512, 512) for i in range(4)]
        mbuf = v2(o4 + 27920, 512)
        drow = vf32(o4 + 28944, 512, 0, 1)
        drow16 = v2(o4 + 29968, 512, 0, 1)
        otmp = v2(o4 + 30480, 512, 0, 65)
        vtm = v3(o4 + 12288, 32, 132)
        qbuf = v2(o4 + 16512, 2048)
        qtmp = v2(o4 + 18560, 2048)
        Tb = v3(o4 + 20608, 2, 2048)
        rk_c = v2(o4 + 24704, 512, 0, 2)
        rq_c = v2(o4 + 25216, 512, 0, 2)
        lnb = vf32(o4 + 25728, 512, 0, 2)
        m2h = [v2(o4 + 26752, 64, 0, 64), v2(o4 + 26816, 64, 64, 128)]
        ksum16 = v2(o4 + 26880, 1)
        Bbuf_h = [v2(o4 + 26896, 512, 0, 64), v2(o4 + 26896, 512, 64, 128)]
        rb65 = v2(o4 + 27408, 512, 0, 65)
        strow = vf32(o4 + 27408, 512)
        sigv = vf32(o4 + 4096, 512)
        musq = vf32(o4 + 8192, 512)
        Trow16 = v2(o4 + 18560, 512)
        avraw = [vf32(o4 + 33568, 512, 0, 65), vf32(o4 + 34592, 512, 0, 65)]

        ep_x32 = [vf32(o4 + 0, 1024), vf32(o4 + 2048, 1024)]
        ep_xf = v2(o4 + 4096, 1024)
        ep_fm = v3(o4 + 5120, 8, 128)
        ep_gl = v2(o4 + 6144, 1024)
        ep_gate = v2(o4 + 7168, 1024)
        ep_ow = v2(o4 + 8192, 1024)
        ep_t1 = v2(o4 + 9216, 1024)
        ep_t2 = vf32(o4 + 10240, 1024)
        wg_sb = v3(o4 + 12288, 8, 1024)
        wo_sb = v3(o4 + 20480, 8, 1024)
        sqt = vf32(o4 + 28672, 1024)
        xin = [vf32(o4 + 12288, 1024), vf32(o4 + 14336, 1024)]
        xf16 = [v2(o4 + 16384, 1024), v2(o4 + 17408, 1024)]
        xln16 = v2(o4 + 18432, 1024)
        f_x2l = v3(o4 + 0, 8, 1024)
        f_lnout = v2(o4 + 8448, 1024)
        f_fmo = v2(o4 + 9472, 1024)
        f_t32 = vf32(o4 + 10496, 128)
        w1stage = [v3(o4 + 31520, 8, 128), v3(o4 + 32544, 8, 128)]
        w2stage = v3(o4 + 0, 32, 128)
        t_coll = v3(o4 + 4096, 32, 128)
        y_stage = vf32_3(o4 + 12288, 8, 1024)
        t_junk = v2(o4 + 12288, 4096)
        t_tr = v2(o4 + 0, 4096)
        t_ftm = v2(o4 + 4096, 128)


        def ln_tile(x32ap, out16ap, wait_extra=()):
            s1 = small[:, 0:1]
            s2 = small[:, 1:2]
            mu = small[:, 2:3]
            m2s = small[:, 3:4]
            rstd = small[:, 4:5]
            w0 = list(wait_extra)
            P.on("vector", lambda v: v.reduce_sum(s1, x32ap, axis=AX),
                 wait=w0 + P.now("act"))
            P.on("scalar", lambda sc: sc.activation(sqt[:, :], x32ap,
                                                    Act.Square),
                 wait=w0 + P.now("dve"), inc="act")
            P.on("vector", lambda v: v.reduce_sum(s2, sqt[:, :], axis=AX),
                 wait=P.now("act"))
            P.on("vector", lambda v: v.tensor_scalar(
                mu, s1, 1.0 / D, None, Alu.mult))
            P.on("vector", lambda v: v.tensor_scalar(
                m2s, s2, 1.0 / D, None, Alu.mult))
            P.on("vector", lambda v: v.tensor_tensor(
                rstd, mu, mu, Alu.mult))
            P.on("vector", lambda v: v.tensor_copy(
                small[:, 15:16], small[:, 14:15]))
            P.on("vector", lambda v: v.tensor_tensor(
                m2s, m2s, rstd, Alu.subtract), inc="dve")
            P.on("scalar", lambda sc: sc.activation(
                rstd, m2s, Act.Ln, bias=cst[:, 0:1]), wait=P.now("dve"))
            P.on("scalar", lambda sc: sc.activation(
                rstd, rstd, Act.Exp, scale=cst[:, 3:4]), inc="act")
            P.on("vector", lambda v: v.tensor_scalar(
                out16ap, x32ap, mu, rstd, Alu.subtract, Alu.mult),
                wait=P.now("act"), inc="dve")

        def bias_sl(i, c):
            return bias5[:, i * 8 + c:i * 8 + c + 1]

        def dump(nm, src_ap):
            if debug and nm in dbg:
                P.on("gpsimd", lambda g, nm=nm, src_ap=src_ap: g.dma_start(
                    out=dbg[nm], in_=src_ap),
                    wait=P.now("dve", "act", "pe", "g", "ld"),
                    inc="g", inc_n=16)

        P.on("gpsimd", lambda g: g.memset(ident[:], 1.0))
        P.on("gpsimd", lambda g: g.affine_select(
            ident[:], ident[:], pattern=[[-1, 128]],
            compare_op=Alu.is_equal, fill=0.0, base=0,
            channel_multiplier=1), inc="gp")
        P.on("vector", lambda v: v.memset(ones_r[:], 1.0))
        P.on("vector", lambda v: v.memset(ones_c[:], 1.0))
        P.on("vector", lambda v: v.memset(bd2[:], 0.0))
        P.on("vector", lambda v: v.memset(bd2[0:64, 0:1], 1.0))
        P.on("vector", lambda v: v.memset(bd2[64:128, 1:2], 1.0))
        P.on("vector", lambda v: v.memset(ones32[:, :], 1.0))
        P.on("vector", lambda v: v.memset(ones_f[:, :], 1.0))
        P.on("vector", lambda v: v.memset(cst[:, 0:1], LN_EPS))
        P.on("vector", lambda v: v.memset(cst[:, 1:2], 1e-24))
        P.on("vector", lambda v: v.memset(cst[:, 2:3], 1e-12))
        P.on("vector", lambda v: v.memset(cst[:, 3:4], -0.5))
        P.on("vector", lambda v: v.memset(cst[:, 4:5], 0.5), inc="dve")
        P.on("tensor", lambda te: te.transpose(
            ps_t1[0:2, :], bd2[:, :], ident[:]),
            wait=P.now("dve") + P.now("gp"), inc="pe")
        P.on("vector", lambda v: v.tensor_copy(bd2T[:, :], ps_t1[0:2, :]),
             wait=P.now("pe"), inc="dve")
        for i, nm in enumerate(("bq", "bk", "bv", "bo", "bg", "b2")):
            for c in range(8):
                P.on("sync", lambda s, nm=nm, i=i, c=c: s.dma_start(
                    out=bias5[:, i * 8 + c:i * 8 + c + 1],
                    in_=bias_d[nm][c * 128:(c + 1) * 128].rearrange("(a b) -> a b", b=1)), inc="ld", inc_n=16)
        for c in range(32):
            P.on("sync", lambda s, c=c: s.dma_start(
                out=b1s[:, c:c + 1],
                in_=b1_d[c * 128:(c + 1) * 128].rearrange("(a b) -> a b", b=1)),
                inc="ld", inc_n=16)
        P.on("sync", lambda s: s.dma_start(
            out=wtrow[:], in_=Wt_d[:, 0].rearrange("(b a) -> b a", b=1)),
            inc="ld", inc_n=16)
        P.on("sync", lambda s: s.dma_start(
            out=tchain[0:1, 0:1], in_=bt_d.rearrange("(b a) -> b a", b=1)),
            inc="ld", inc_n=16)

        ntile = S // 128
        for t in range(ntile + Q // 128):
            is_q = t >= ntile
            ti = t - ntile if is_q else t
            src = xq_d if is_q else xkv
            dstT = xT_LNq if is_q else xT_LN
            bi = t % 2
            P.on("sync", lambda s, src=src, ti=ti, bi=bi: s.dma_start(
                out=xin[bi], in_=src[ti * 128:(ti + 1) * 128, :]),
                wait=P.now("dve"), inc="ld", inc_n=16)
            if not is_q:
                P.on("gpsimd", lambda g, ti=ti, bi=bi: g.dma_start(
                    out=xf16[bi], in_=xkv[ti * 128:(ti + 1) * 128, :]),
                    wait=P.now("pe"), inc="g", inc_n=16)
            ln_tile(xin[bi], xln16[:, :],
                    wait_extra=P.now("ld") + P.now("pe"))
            if not is_q:
                gw = P.now("g")
                for half, ps in ((0, ps_c), (1, ps_d)):
                    P.on("tensor", lambda te, bi=bi, half=half, ps=ps, ti=ti:
                         te.matmul(ps[0:1, :], ones_c[:, :],
                                   xf16[bi][:, half * 512:(half + 1) * 512],
                                   start=(ti == 0), stop=(ti == ntile - 1),
                                   skip_group_check=True),
                         wait=gw, inc="pe")
            dw = P.now("dve")
            for c in range(8):
                pst = ps_t1 if c % 2 == 0 else ps_t2
                P.on("tensor", lambda te, c=c, pst=pst: te.transpose(
                    pst[:], xln16[:, c * 128:(c + 1) * 128], ident[:]),
                    wait=dw + (P.now("dve") if c >= 2 else []), inc="pe")
                P.on("vector", lambda v, c=c, ti=ti, pst=pst, dstT=dstT:
                     v.tensor_copy(dstT[:, c, ti * 128:(ti + 1) * 128],
                                   pst[:]), wait=P.now("pe"), inc="dve")
        P.on("vector", lambda v: v.tensor_copy(tmp1[0:1, 0:512], ps_c[0:1, :]),
             wait=P.now("pe"))
        P.on("vector", lambda v: v.tensor_copy(tmp1[0:1, 512:1024],
                                               ps_d[0:1, :]))
        P.on("vector", lambda v: v.tensor_tensor(
            tmp1[:], tmp1[:], wtrow[:], Alu.mult))
        P.on("vector", lambda v: v.reduce_sum(
            tchain[0:1, 1:2], tmp1[:], axis=AX), inc="dve")
        P.on("vector", lambda v: v.tensor_copy(
            small[:, 15:16], small[:, 14:15]))
        P.on("vector", lambda v: v.tensor_scalar(
            tchain[0:1, 2:3], tchain[0:1, 1:2], 1.0 / S, tchain[0:1, 0:1],
            Alu.mult, Alu.add), wait=P.now("ld"))
        P.on("scalar", lambda sc: sc.activation(
            tchain[0:1, 3:4], tchain[0:1, 2:3], Act.Sigmoid),
            wait=P.now("dve"), inc="act")
        P.on("vector", lambda v: v.tensor_scalar(
            tchain[0:1, 4:5], tchain[0:1, 3:4], 0.5, None, Alu.add),
            wait=P.now("act"))
        P.on("vector", lambda v: v.tensor_copy(
            small[:, 15:16], small[:, 14:15]))
        P.on("vector", lambda v: v.reciprocal(
            tchain[0:1, 5:6], tchain[0:1, 4:5]), inc="dve")
        P.on("tensor", lambda te: te.matmul(
            ps_d[:, 0:1], ones32[:, :], tchain[0:1, 5:6],
            start=True, stop=True), wait=P.now("dve"), inc="pe")
        P.on("vector", lambda v: v.tensor_copy(rtempb[:, :], ps_d[:, 0:1]),
             wait=P.now("pe"), inc="dve")

        dump("dbg_xtln", xT_LN[:, 0, :])
        dump("dbg_tchain", tchain[:, :])
        lastc = {}

        def use_ps(key):
            return [w for w in (lastc.get(key),) if w]

        def took(key, sem):
            lastc[key] = (sem, P.cnt(sem))

        pend_norm = []
        avr_done = {}

        def ph1(hp):
            c0 = hp * 128
            def norm_flush(hp, cur_h2):
                h2, qc, ai = pend_norm.pop(0)
                bps = ps_av2 if cur_h2 == 0 else ps_av1
                P.on("vector", lambda v, ai=ai: v.reciprocal(
                    drow[:, :], avraw[ai][0:1, :]),
                    wait=[("dve", avr_done[(h2, qc)])])
                P.on("vector", lambda v: v.tensor_copy(
                    drow16[:, :], drow[:, :]),
                    wait=use_ps("otmp"), inc="dve")
                P.on("tensor", lambda te, bps=bps: te.matmul(
                    bps[0:65, :], ones_r[:, 0:65], drow16[:, :],
                    start=True, stop=True),
                    wait=P.now("dve") + use_ps(("avb", id(bps))), inc="pe")
                P.on("vector", lambda v, bps=bps: v.tensor_copy(
                    rb65[:, :], bps[0:65, :]), wait=P.now("pe"),
                    inc="dve")
                took(("avb", id(bps)), "dve")
                P.on("vector", lambda v, ai=ai: v.tensor_tensor(
                    otmp[:, :], avraw[ai][:, :], rb65[:, :], Alu.mult),
                    inc="dve")
                took(("avr", ai), "dve")
                P.on("gpsimd", lambda g, hp=hp, h2=h2, qc=qc:
                     g.dma_start(
                         out=O_all[h2 * 64:(h2 + 1) * 64, hp,
                                   qc * 512:(qc + 1) * 512],
                         in_=otmp[1:65, :]),
                     wait=P.now("dve"), inc="g", inc_n=16)
                took("otmp", "g")

            if hp == 0:
                P.on("vector", lambda v: v.memset(vtm[:, :, 0:1], 1.0),
                     wait=P.now("ld", "g", "pe"))
                P.on("vector", lambda v: v.memset(vtm[:, :, 66:67], 1.0),
                     inc="dve")
            for wi, wsrc in ((0, Wq), (1, Wk), (2, Wv)):
                P.on("gpsimd", lambda g, wi=wi, wsrc=wsrc, c0=c0:
                     g.dma_start(
                         out=wsl[:, wi * 8:(wi + 1) * 8, :],
                         in_=wsrc[:, c0:c0 + 128].rearrange(
                             "(dc p) f -> p dc f", p=128)),
                     wait=use_ps("wsl") if wi == 0 else [],
                     inc="g", inc_n=16)
            gw = P.now("g")

            def proj(kind, nsl, wbase, dst_fm, bias_i, xsrc, nchunk):
                pend = []

                def norm_tail(kc):
                    sq_src = ktmp if kind == "k" else qtmp
                    P.on("tensor", lambda te, kc=kc, sq_src=sq_src:
                         te.matmul(
                             ps_c[0:2, :], bd2[:, :],
                             sq_src[:, kc * 512:(kc + 1) * 512],
                             start=True, stop=True),
                         wait=[("act", sq_done[kc])] + use_ps("ps_c2"),
                         inc="pe")
                    P.on("scalar", lambda sc: sc.activation(
                        lnb[:, :], ps_c[0:2, :], Act.Ln,
                        bias=cst[0:2, 1:2]),
                        wait=P.now("pe") + use_ps("lnb"), inc="act")
                    took("ps_c2", "act")
                    rr = rk_c if kind == "k" else rq_c
                    P.on("scalar", lambda sc, rr=rr: sc.activation(
                        rr[:, :], lnb[:, :], Act.Exp,
                        scale=cst[0:2, 3:4]), inc="act")
                    took("lnb", "act")
                    if kind == "q":
                        P.on("vector", lambda v: v.tensor_scalar(
                            rq_c[:, :], rq_c[:, :], rtempb[0:2, 0:1],
                            None, Alu.mult), wait=P.now("act"),
                            inc="dve")
                    P.on("tensor", lambda te, rr=rr: te.matmul(
                        ps_d[:, :], bd2T[:, :], rr[:, :],
                        start=True, stop=True),
                        wait=(P.now("dve") if kind == "q"
                              else P.now("act")) + use_ps("ps_dn"),
                        inc="pe")
                    P.on("vector", lambda v, kc=kc: v.tensor_tensor(
                        dst_fm[:, kc * 512:(kc + 1) * 512],
                        dst_fm[:, kc * 512:(kc + 1) * 512],
                        ps_d[:, :], Alu.mult),
                        wait=P.now("pe"), inc="dve")
                    took("ps_dn", "dve")

                sq_done = {}
                for kc in range(nchunk):
                    ps = ps_a if kc % 2 == 0 else ps_b
                    for dc in range(8):
                        P.on("tensor", lambda te, kc=kc, dc=dc, ps=ps:
                             te.matmul(
                                 ps[:], wsl[:, wbase + dc, :],
                                 xsrc[:, dc, kc * 512:(kc + 1) * 512],
                                 start=(dc == 0), stop=(dc == 7)),
                             wait=((gw if kind == "k" else []) +
                                   use_ps(id(ps)) if dc == 0 else []),
                             inc="pe")
                    pw = P.now("pe")
                    P.on("scalar", lambda sc, kc=kc, ps=ps: sc.activation(
                        dst_fm[:, kc * 512:(kc + 1) * 512], ps[:],
                        Act.Identity, bias=bias_sl(bias_i, hp)),
                        wait=pw, inc="act")
                    if kind != "v":
                        sq_dst = ktmp if kind == "k" else qtmp
                        P.on("scalar", lambda sc, kc=kc, ps=ps, sq_dst=sq_dst:
                             sc.activation(
                                 sq_dst[:, kc * 512:(kc + 1) * 512], ps[:],
                                 Act.Square, bias=bias_sl(bias_i, hp)),
                             inc="act")
                        sq_done[kc] = P.cnt("act")
                    took(id(ps), "act")
                    if kind != "v":
                        pend.append(kc)
                        if len(pend) >= 2:
                            norm_tail(pend.pop(0))
                for kc in pend:
                    norm_tail(kc)

            proj("k", None, 8, kbuf, 1, xT_LN, 8)
            proj("q", None, 0, qbuf, 0, xT_LNq, 4)
            proj("v", None, 16, ktmp, 2, xT_LN, 8)
            took("wsl", "pe")

            for src_buf, kind in ((ktmp, "v"), (kbuf, "kn")):
                for kt in range(32):
                    pst = ps_t1 if kt % 2 == 0 else ps_t2
                    P.on("tensor", lambda te, kt=kt, pst=pst, src_buf=src_buf:
                         te.transpose(
                             pst[:], src_buf[:, kt * 128:(kt + 1) * 128],
                             ident[:]),
                         wait=(P.now("act" if kind == "v" else "dve")
                               if kt == 0 else []) + use_ps(id(pst)),
                         inc="pe")
                    pw = P.now("pe")
                    if kind == "v":
                        P.on("vector", lambda v, kt=kt, pst=pst:
                             v.tensor_copy(vtm[:, kt, 1:65], pst[:, 0:64]),
                             wait=pw)
                        P.on("vector", lambda v, kt=kt, pst=pst:
                             v.tensor_copy(vtm[:, kt, 67:131],
                                           pst[:, 64:128]), inc="dve")
                    else:
                        P.on("vector", lambda v, kt=kt, pst=pst:
                             v.tensor_copy(ktm[:, kt, :], pst[:]),
                             wait=pw, inc="dve")
                    took(id(pst), "dve")

            for h2, psx in ((0, ps_c), (1, ps_d)):
                for kt in range(32):
                    P.on("tensor", lambda te, h2=h2, kt=kt, psx=psx:
                         te.matmul(psx[h2 * 64:(h2 + 1) * 64, 0:64],
                                   ktm[:, kt, h2 * 64:(h2 + 1) * 64],
                                   ktm[:, kt, h2 * 64:(h2 + 1) * 64],
                                   start=(kt == 0), stop=(kt == 31),
                                   tile_position=(0, h2 * 64)),
                         wait=((P.now("dve") + use_ps("ps_c2") +
                                use_ps("ps_dn")) if kt == 0 and h2 == 0
                               else []),
                         inc="pe")
            for kt in range(32):
                P.on("tensor", lambda te, kt=kt: te.matmul(
                    ps_a[:, 0:1], ktm[:, kt, :], ones_c[:, :],
                    start=(kt == 0), stop=(kt == 31)),
                    wait=(use_ps(id(ps_a)) if kt == 0 else []), inc="pe")
            pw = P.now("pe")
            P.on("vector", lambda v: v.tensor_copy(m2h[0][:, :],
                                                   ps_c[0:64, 0:64]), wait=pw)
            P.on("vector", lambda v: v.tensor_copy(m2h[1][:, :],
                                                   ps_d[64:128, 0:64]))
            P.on("vector", lambda v: v.tensor_copy(ksum16[:, :],
                                                   ps_a[:, 0:1]), inc="dve")
            took(id(ps_a), "dve")
            took("ps_c2", "dve")
            took("ps_dn", "dve")

            for h2 in range(2):
                r0 = h2 * 64
                psv = ps_av1 if h2 == 0 else ps_av2
                for qc in range(4):
                    qsl = qbuf[r0:r0 + 64, qc * 512:(qc + 1) * 512]
                    p4 = 32 * qc
                    P.on("tensor", lambda te, h2=h2, r0=r0, qsl=qsl:
                         te.matmul(
                             ps_a[r0:r0 + 64, :], m2h[h2][:, :], qsl,
                             start=True, stop=True, tile_position=(r0, r0)),
                         wait=P.now("dve") + use_ps(id(ps_a)), inc="pe")
                    P.on("vector", lambda v, h2=h2, r0=r0, qsl=qsl:
                         v.tensor_tensor(
                             Bbuf_h[h2][:, :], qsl, ps_a[r0:r0 + 64, :],
                             Alu.mult), wait=P.now("pe"), inc="dve")
                    took(id(ps_a), "dve")
                    P.on("tensor", lambda te, r0=r0, p4=p4, qsl=qsl:
                         te.matmul(
                             ps_b[p4:p4 + 1, :], ksum16[r0:r0 + 64, :], qsl,
                             start=True, stop=True, tile_position=(r0, p4)),
                         wait=P.now("dve") +
                         (use_ps(id(ps_b)) if qc == 0 else []), inc="pe")
                    P.on("tensor", lambda te, h2=h2, r0=r0, p4=p4:
                         te.matmul(
                             ps_c[p4:p4 + 1, :], ones_c[r0:r0 + 64, :],
                             Bbuf_h[h2][:, :],
                             start=True, stop=True, tile_position=(r0, p4)),
                         wait=(use_ps(id(ps_c)) if qc == 0 else []),
                         inc="pe")
                pw = P.now("pe")
                P.on("vector", lambda v: v.tensor_copy(
                    strow[:, :], ps_b[:, :]), wait=pw + use_ps("otmp"))
                took(id(ps_b), "dve")
                P.on("vector", lambda v: v.tensor_copy(
                    sigv[:, :], ps_c[:, :]), inc="dve")
                took(id(ps_c), "dve")

                def warm(n, w):
                    for i in range(n):
                        P.on("tensor", lambda te: te.matmul(
                            ps_a[0:64, :], m2h[0][:, :],
                            qbuf[0:64, 0:512], start=True, stop=True),
                            wait=(w if i == 0 else []))
                warm(10, P.now("dve"))
                P.on("vector", lambda v: v.tensor_scalar(
                    strow[:, :], strow[:, :], 1.0 / S, None, Alu.mult),
                    wait=P.now("dve"))
                P.on("vector", lambda v: v.tensor_scalar(
                    sigv[:, :], sigv[:, :], 1.0 / S, None, Alu.mult))
                P.on("vector", lambda v: v.tensor_tensor(
                    musq[:, :], strow[:, :], strow[:, :], Alu.mult))
                P.on("vector", lambda v: v.tensor_copy(
                    small[:, 15:16], small[:, 14:15]))
                P.on("vector", lambda v: v.tensor_tensor(
                    sigv[:, :], sigv[:, :], musq[:, :], Alu.subtract),
                    inc="dve")
                warm(4, P.now("dve"))
                P.on("scalar", lambda sc: sc.activation(
                    sigv[:, :], sigv[:, :], Act.Ln, bias=cst[:, 2:3]),
                    wait=P.now("dve"))
                P.on("scalar", lambda sc: sc.activation(
                    sigv[:, :], sigv[:, :], Act.Exp,
                    scale=cst[:, 4:5]), inc="act")
                warm(6, P.now("act"))
                P.on("vector", lambda v: v.tensor_scalar(
                    sigv[:, :], sigv[:, :], ZINIT, None, Alu.mult),
                    wait=P.now("act"))
                P.on("vector", lambda v: v.tensor_copy(
                    small[:, 15:16], small[:, 14:15]))
                P.on("vector", lambda v: v.tensor_tensor(
                    strow[:, :], strow[:, :], sigv[:, :], Alu.add),
                    inc="dve")
                warm(4, P.now("dve"))
                P.on("scalar", lambda sc: sc.activation(
                    strow[:, :], strow[:, :], Act.Exp),
                    wait=P.now("dve"), inc="act")
                P.on("vector", lambda v: v.tensor_copy(
                    Trow16[:, :], strow[:, :]), wait=P.now("act"),
                    inc="dve")
                for qc in range(4):
                    P.on("tensor", lambda te, qc=qc: te.matmul(
                        ps_d[:, :], ones_f[32 * qc:32 * qc + 1, :],
                        Trow16[32 * qc:32 * qc + 1, :],
                        start=True, stop=True, tile_position=(32 * qc, 0)),
                        wait=(P.now("dve") if qc == 0 else []) +
                        use_ps(id(ps_d)), inc="pe")
                    P.on("vector", lambda v, h2=h2, qc=qc: v.tensor_copy(
                        Tb[:, h2, qc * 512:(qc + 1) * 512], ps_d[:, :]),
                        wait=P.now("pe"), inc="dve")
                    took(id(ps_d), "dve")

                rings = (ps_a, ps_b, ps_c, ps_d)

                def emit_mm(kt, qc, qsl):
                    pss = rings[kt % 4]
                    P.on("tensor", lambda te, kt=kt, pss=pss, qsl=qsl,
                         r0=r0: te.matmul(
                             pss[:, :],
                             kbuf[r0:r0 + 64, kt * 128:(kt + 1) * 128],
                             qsl, start=True, stop=True,
                             tile_position=(r0, 0)),
                         wait=(P.now("dve") if kt == 0 and qc == 0
                               else []) + use_ps(id(pss)), inc="pe")

                def emit_maskw(kt, qc):
                    pss = rings[kt % 4]
                    ei = kt % 4
                    P.on("scalar", lambda sc, pss=pss, ei=ei:
                         sc.activation(Et[ei][:, :], pss[:, :], Act.Exp),
                         wait=[("pe", pe_after_mm[kt])] +
                         use_ps(("et", ei)), inc="act")
                    took(id(pss), "act")
                    P.on("vector", lambda v, ei=ei, h2=h2, qc=qc:
                         v.tensor_tensor(
                             mbuf[:, :], Et[ei][:, :],
                             Tb[:, h2, qc * 512:(qc + 1) * 512],
                             Alu.is_ge),
                         wait=P.now("act") + use_ps(("wb", ei)))
                    P.on("vector", lambda v, ei=ei: v.tensor_tensor(
                        wb[ei][:, :], mbuf[:, :], Et[ei][:, :],
                        Alu.mult), inc="dve")
                    took(("et", ei), "dve")
                    dve_after_w[kt] = P.cnt("dve")

                def emit_av(kt):
                    ei = kt % 4
                    st_sl = (vtm[:, kt, 0:65] if h2 == 0
                             else vtm[:, kt, 66:131])
                    P.on("tensor", lambda te, kt=kt, psv=psv,
                         st_sl=st_sl: te.matmul(
                             psv[:, :], st_sl, wb[ei][:, :],
                             start=(kt == 0), stop=(kt == 31)),
                         wait=[("dve", dve_after_w[kt])] +
                         (use_ps(("avb", id(psv))) if kt == 0 else []),
                         inc="pe")
                    took(("wb", ei), "pe")

                for qc in range(4):
                    qsl = qbuf[r0:r0 + 64, qc * 512:(qc + 1) * 512]
                    pe_after_mm = {}
                    dve_after_w = {}
                    for kt in range(4):
                        emit_mm(kt, qc, qsl)
                        pe_after_mm[kt] = P.cnt("pe")
                    for kt in range(32):
                        emit_maskw(kt, qc)
                        if kt > 0:
                            emit_av(kt - 1)
                        if kt + 4 < 32:
                            emit_mm(kt + 4, qc, qsl)
                            pe_after_mm[kt + 4] = P.cnt("pe")
                    emit_av(31)
                    ai = (h2 * 4 + qc) % 2
                    P.on("vector", lambda v, psv=psv, ai=ai: v.tensor_copy(
                        avraw[ai][:, :], psv[0:65, :]),
                        wait=P.now("pe") + use_ps(("avr", ai)), inc="dve")
                    avr_done[(h2, qc)] = P.cnt("dve")
                    pend_norm.append((h2, qc, ai))
                    norm_flush(hp, h2)


        def ph_wo():
            for dc in range(8):
                P.on("gpsimd", lambda g, dc=dc: g.dma_start(
                    out=wo_sb[:, dc, :], in_=Wo_d[dc * 128:(dc + 1) * 128, :]),
                    wait=P.now("dve"), inc="g", inc_n=16)
            gw = P.now("g")
            for dc in range(8):
                for qc in range(4):
                    ps = ps_a if (dc * 4 + qc) % 2 == 0 else ps_b
                    for hp in range(NHP):
                        P.on("tensor", lambda te, dc=dc, qc=qc, hp=hp, ps=ps:
                             te.matmul(
                                 ps[:], wo_sb[:, hp, dc * 128:(dc + 1) * 128],
                                 O_all[:, hp, qc * 512:(qc + 1) * 512],
                                 start=(hp == 0), stop=(hp == 7)),
                             wait=(gw if dc == 0 and qc == 0 and hp == 0
                                   else (P.now("act") if hp == 0 else [])),
                             inc="pe")
                    P.on("scalar", lambda sc, dc=dc, qc=qc, ps=ps:
                         sc.activation(
                             owo_fm[:, dc, qc * 512:(qc + 1) * 512], ps[:],
                             Act.Identity, bias=bias_sl(3, dc)),
                         wait=P.now("pe"), inc="act")

        def ph_epi():
            for dc in range(8):
                P.on("gpsimd", lambda g, dc=dc: g.dma_start(
                    out=wg_sb[:, dc, :], in_=Wg_d[dc * 128:(dc + 1) * 128, :]),
                    wait=(P.now("dve") if dc == 0 else []), inc="g",
                    inc_n=16)
            gw = P.now("g")
            for t in range(Q // 128):
                bi = t % 2
                P.on("sync", lambda s, t=t, bi=bi: s.dma_start(
                    out=ep_x32[bi], in_=xq_d[t * 128:(t + 1) * 128, :]),
                    wait=P.now("dve"), inc="ld", inc_n=16)
                ln_tile(ep_x32[bi], ep_xf[:, :],
                        wait_extra=P.now("ld") + (gw if t == 0 else []))
                dw = P.now("dve")
                for c in range(8):
                    pst = ps_t1 if c % 2 == 0 else ps_t2
                    P.on("tensor", lambda te, c=c, pst=pst: te.transpose(
                        pst[:], ep_xf[:, c * 128:(c + 1) * 128], ident[:]),
                        wait=dw + (P.now("dve") if c >= 2 else []), inc="pe")
                    P.on("vector", lambda v, c=c, pst=pst: v.tensor_copy(
                        ep_fm[:, c, :], pst[:]), wait=P.now("pe"),
                        inc="dve")
                dw = P.now("dve")
                for dc in range(8):
                    ps = ps_a if dc % 2 == 0 else ps_b
                    for di in range(8):
                        P.on("tensor", lambda te, dc=dc, di=di, ps=ps:
                             te.matmul(
                                 ps[:, 0:128],
                                 wg_sb[:, di, dc * 128:(dc + 1) * 128],
                                 ep_fm[:, di, :],
                                 start=(di == 0), stop=(di == 7)),
                             wait=((dw + P.now("act")) if di == 0 else []),
                             inc="pe")
                    P.on("scalar", lambda sc, dc=dc, ps=ps: sc.activation(
                        ep_t1[:, 0:128], ps[:, 0:128], Act.Identity,
                        bias=bias_sl(4, dc)), wait=P.now("pe"), inc="act")
                    pst = ps_t1 if dc % 2 == 0 else ps_t2
                    P.on("tensor", lambda te, pst=pst: te.transpose(
                        pst[:], ep_t1[:, 0:128], ident[:]),
                        wait=P.now("act"), inc="pe")
                    P.on("vector", lambda v, dc=dc, pst=pst: v.tensor_copy(
                        ep_gl[:, dc * 128:(dc + 1) * 128], pst[:]),
                        wait=P.now("pe"), inc="dve")
                P.on("scalar", lambda sc: sc.activation(
                    ep_gate[:, :], ep_gl[:, :], Act.Sigmoid),
                    wait=P.now("dve"), inc="act")
                for c in range(8):
                    pst = ps_t1 if c % 2 == 0 else ps_t2
                    P.on("tensor", lambda te, c=c, t=t, pst=pst: te.transpose(
                        pst[:], owo_fm[:, c, t * 128:(t + 1) * 128],
                        ident[:]), wait=P.now("dve"), inc="pe")
                    P.on("vector", lambda v, c=c, pst=pst: v.tensor_copy(
                        ep_ow[:, c * 128:(c + 1) * 128], pst[:]),
                        wait=P.now("pe"), inc="dve")
                P.on("vector", lambda v: v.tensor_tensor(
                    ep_t1[:, :], ep_ow[:, :], ep_xf[:, :], Alu.subtract),
                    wait=P.now("act"))
                P.on("vector", lambda v: v.tensor_tensor(
                    ep_t1[:, :], ep_gate[:, :], ep_t1[:, :], Alu.mult))
                P.on("vector", lambda v: v.tensor_tensor(
                    ep_t1[:, :], ep_t1[:, :], ep_xf[:, :], Alu.add))
                P.on("vector", lambda v: v.tensor_copy(
                    ep_t2[:, :], ep_t1[:, :]))
                P.on("vector", lambda v, t=t, bi=bi: v.tensor_tensor(
                    x2[:, t, :], ep_x32[bi], ep_t2[:, :], Alu.add),
                    inc="dve")

        def ph_tern():
            for wi, (wsrc, nchunk, width) in enumerate(
                    ((W1_d, 32, 1024), (W2_d, 8, 4096))):
                P.on("vector", lambda v, wi=wi: v.memset(
                    accw[:, wi:wi + 1], 0.0), inc="dve")
                for ch in range(nchunk):
                    bi = ch % 2
                    P.on("sync", lambda s, wsrc=wsrc, ch=ch, bi=bi,
                         width=width: s.dma_start(
                             out=tw32[bi][:, 0:width],
                             in_=wsrc[ch * 128:(ch + 1) * 128, :]),
                         wait=use_ps(("tw", bi)), inc=f"tw{bi}", inc_n=16)
                    P.on("scalar", lambda sc, bi=bi, width=width:
                         sc.activation(t_junk[:, 0:width],
                                       tw32[bi][:, 0:width], Act.Abs,
                                       accum_out=small[:, 6:7]),
                         wait=P.now(f"tw{bi}") + P.now("dve"), inc="act")
                    took(("tw", bi), "act")
                    P.on("vector", lambda v: v.tensor_copy(
                        small[:, 15:16], small[:, 14:15]))
                    P.on("vector", lambda v, wi=wi: v.tensor_tensor(
                        accw[:, wi:wi + 1], accw[:, wi:wi + 1],
                        small[:, 6:7], Alu.add), wait=P.now("act"),
                        inc="dve")
                P.on("gpsimd", lambda g, wi=wi: g.dma_start(
                    out=scr_h[wi * 128:(wi + 1) * 128].rearrange(
                        "(a b) -> a b", a=128),
                    in_=accw[:, wi:wi + 1]),
                    wait=P.now("dve"), inc="g", inc_n=16)
                P.on("gpsimd", lambda g, wi=wi: g.dma_start(
                    out=tmp1[0:1, wi * 128:(wi + 1) * 128],
                    in_=scr_h[wi * 128:(wi + 1) * 128].rearrange(
                        "(a b) -> a b", a=1)),
                    wait=P.now("g"), inc="g", inc_n=16)
                P.on("vector", lambda v, wi=wi: v.reduce_sum(
                    tchain[0:1, 6 + wi:7 + wi],
                    tmp1[0:1, wi * 128:(wi + 1) * 128], axis=AX),
                    wait=P.now("g"))
                P.on("vector", lambda v: v.tensor_copy(
                    small[:, 15:16], small[:, 14:15]))
                P.on("vector", lambda v, wi=wi: v.tensor_scalar(
                    tchain[0:1, 6 + wi:7 + wi], tchain[0:1, 6 + wi:7 + wi],
                    1.0 / (4096.0 * 1024.0), 1e-8, Alu.mult, Alu.max))
                P.on("vector", lambda v: v.tensor_copy(
                    small[:, 15:16], small[:, 14:15]))
                P.on("vector", lambda v, wi=wi: v.reciprocal(
                    tchain[0:1, 6 + wi:7 + wi], tchain[0:1, 6 + wi:7 + wi]),
                    inc="dve")
                P.on("tensor", lambda te, wi=wi: te.matmul(
                    ps_d[:, 0:1], ones32[:, :], tchain[0:1, 6 + wi:7 + wi],
                    start=True, stop=True), wait=P.now("dve"), inc="pe")
                P.on("vector", lambda v, wi=wi: v.tensor_copy(
                    rs12[:, wi:wi + 1], ps_d[:, 0:1]),
                    wait=P.now("pe"), inc="dve")
            gw = P.now("g")
            for wi, (wsrc, dsth, nchunk, width) in enumerate(
                    ((W1_d, w1tT_h, 32, 1024), (W2_d, w2tT_h, 8, 4096))):
                for ch in range(nchunk):
                    bi = ch % 2
                    P.on("sync", lambda s, wsrc=wsrc, ch=ch, bi=bi,
                         width=width: s.dma_start(
                             out=tw32[bi][:, 0:width],
                             in_=wsrc[ch * 128:(ch + 1) * 128, :]),
                         wait=(gw if ch < 2 and wi == 0 else []) +
                         use_ps(("tw", bi)), inc=f"tw{bi}", inc_n=16)
                    lw = P.now(f"tw{bi}")
                    P.on("vector", lambda v, bi=bi, wi=wi, width=width:
                         v.tensor_scalar(
                             tw32[bi][:, 0:width], tw32[bi][:, 0:width],
                             rs12[:, wi:wi + 1], MAGIC, Alu.mult, Alu.add),
                         wait=lw)
                    P.on("vector", lambda v, bi=bi, width=width:
                         v.tensor_scalar(
                             tw32[bi][:, 0:width], tw32[bi][:, 0:width],
                             MAGIC, -1.0, Alu.subtract, Alu.max))
                    P.on("vector", lambda v, bi=bi, width=width:
                         v.tensor_scalar(
                             t_tr[:, 0:width], tw32[bi][:, 0:width],
                             1.0, None, Alu.min),
                         wait=use_ps("t_tr"), inc="dve")
                    took(("tw", bi), "dve")
                    dw = P.now("dve")
                    for tt in range(width // 128):
                        pst = ps_t1 if tt % 2 == 0 else ps_t2
                        P.on("tensor", lambda te, tt=tt, pst=pst:
                             te.transpose(pst[:],
                                          t_tr[:, tt * 128:(tt + 1) * 128],
                                          ident[:]),
                             wait=(dw if tt < 2 else []) + use_ps(id(pst)),
                             inc="pe")
                        P.on("vector", lambda v, tt=tt, pst=pst:
                             v.tensor_copy(t_coll[:, tt, :], pst[:]),
                             wait=P.now("pe") + (P.now("g") if tt == 0
                                                 else []), inc="dve")
                        took(id(pst), "dve")
                    took("t_tr", "pe")
                    P.on("gpsimd", lambda g, dsth=dsth, ch=ch, width=width:
                         g.dma_start(
                             out=dsth[ch],
                             in_=t_coll[:, 0:width // 128, :]),
                         wait=P.now("dve"), inc="g", inc_n=16)

        def ph_ffn():
            for tb in range(2):
                P.barrier("vector", ["g", "ld"])
                for tt in range(8):
                    t = tb * 8 + tt
                    ln_tile(x2[:, t, :], f_lnout[:, :],
                            wait_extra=P.now("pe"))
                    dw = P.now("dve")
                    for c in range(8):
                        pst = ps_t1 if c % 2 == 0 else ps_t2
                        P.on("tensor", lambda te, c=c, pst=pst: te.transpose(
                            pst[:], f_lnout[:, c * 128:(c + 1) * 128],
                            ident[:]),
                            wait=dw + (P.now("dve") if c >= 2 else []),
                            inc="pe")
                        P.on("vector", lambda v, c=c, tt=tt, pst=pst:
                             v.tensor_copy(
                                 f_x2l[:, c, tt * 128:(tt + 1) * 128],
                                 pst[:]), wait=P.now("pe"), inc="dve")
                if tb == 1:
                    dump("dbg_eg", f_x2l[:, 0, 0:512])
                    dump("dbg_x2b", x2[:, 8:16, :].rearrange("p a b -> p (a b)"))
                    dump("dbg_x2l", f_x2l.rearrange("p a b -> p (a b)"))
                for fc in range(32):
                    p0 = ps_a if fc % 2 == 0 else ps_b
                    p1 = ps_c if fc % 2 == 0 else ps_d
                    sbi = fc % 2
                    P.on("sync", lambda s, fc=fc, sbi=sbi: s.dma_start(
                        out=w1stage[sbi][:, :, :],
                        in_=w1tT_h[fc]),
                        wait=use_ps(("w1s", sbi)),
                        inc=f"w1{sbi}", inc_n=16)
                    lw = P.now(f"w1{sbi}")
                    for di in range(8):
                        for half, psx in ((0, p0), (1, p1)):
                            P.on("tensor", lambda te, di=di, psx=psx,
                                 half=half, sbi=sbi: te.matmul(
                                     psx[:, :], w1stage[sbi][:, di, :],
                                     f_x2l[:, di,
                                           half * 512:(half + 1) * 512],
                                     start=(di == 0), stop=(di == 7)),
                                 wait=(lw if di == 0 and half == 0 else
                                       []) + (use_ps(id(psx))
                                              if di == 0 else []),
                                 inc="pe")
                    if tb == 1 and fc == 1:
                        dump("dbg_w1s", w1stage[sbi].rearrange(
                            "p a b -> p (a b)"))
                    pw = P.now("pe")
                    for half, psx in ((0, p0), (1, p1)):
                        P.on("scalar", lambda sc, fc=fc, half=half, psx=psx:
                             sc.activation(
                                 hg[:, fc, half * 512:(half + 1) * 512],
                                 psx[:, :], Act.Gelu,
                                 bias=b1s[:, fc:fc + 1]),
                             wait=(pw if half == 0 else []), inc="act")
                        took(id(psx), "act")
                    took(("w1s", sbi), "act")
                if tb == 1:
                    dump("dbg_tb", hg[:, 0:4, :].rearrange("p a b -> p (a b)"))
                for dc in range(8):
                    p0 = ps_a if dc % 2 == 0 else ps_b
                    p1 = ps_c if dc % 2 == 0 else ps_d
                    P.on("sync", lambda s, dc=dc: s.dma_start(
                        out=w2stage[:, :, :],
                        in_=w2tT_h[dc]),
                        wait=use_ps("w2st") + (P.now("act", "dve")
                                               if dc == 0 else []),
                        inc="w2s", inc_n=16)
                    lw = P.now("w2s")
                    for fc in range(32):
                        for half, psx in ((0, p0), (1, p1)):
                            P.on("tensor", lambda te, fc=fc, psx=psx,
                                 half=half: te.matmul(
                                     psx[:, :], w2stage[:, fc, :],
                                     hg[:, fc, half * 512:(half + 1) * 512],
                                     start=(fc == 0), stop=(fc == 31)),
                                 wait=(lw + P.now("act", "dve")
                                       if fc == 0 and half == 0 else
                                       (use_ps(id(psx)) if fc == 0
                                        else [])), inc="pe")
                    took("w2st", "pe")
                    pw = P.now("pe")
                    for half, psx in ((0, p0), (1, p1)):
                        P.on("scalar", lambda sc, dc=dc, half=half, psx=psx:
                             sc.activation(
                                 f_fmo[:, half * 512:(half + 1) * 512],
                                 psx[:, :], Act.Identity,
                                 bias=bias_sl(5, dc)),
                             wait=(pw if half == 0 else []), inc="act")
                        took(id(psx), "act")
                    aw = P.now("act")
                    for tt in range(8):
                        pst = ps_t1 if tt % 2 == 0 else ps_t2
                        P.on("tensor", lambda te, tt=tt, pst=pst:
                             te.transpose(
                                 pst[:], f_fmo[:, tt * 128:(tt + 1) * 128],
                                 ident[:]),
                             wait=aw + (P.now("dve") if tt >= 2 else []),
                             inc="pe")
                        t = tb * 8 + tt
                        P.on("vector", lambda v, pst=pst: v.tensor_copy(
                            f_t32[:, :], pst[:]), wait=P.now("pe"))
                        P.on("vector", lambda v, t=t, tt=tt, dc=dc:
                             v.tensor_tensor(
                                 y_stage[:, tt, dc * 128:(dc + 1) * 128],
                                 x2[:, t, dc * 128:(dc + 1) * 128],
                                 f_t32[:, :], Alu.add), inc="dve")
                dw = P.now("dve")
                for tt in range(8):
                    t = tb * 8 + tt
                    P.on("gpsimd", lambda g, t=t, tt=tt: g.dma_start(
                        out=y_d[t * 128:(t + 1) * 128, :],
                        in_=y_stage[:, tt, :]),
                        wait=dw, inc="g", inc_n=16)

        for hp in range(NHP):
            ph1(hp)
        dump("dbg_oall", O_all.rearrange("p a b -> p (a b)"))
        P.barrier("tensor", ["dve", "act", "g"])
        P.barrier("gpsimd", ["dve", "act", "pe"])
        ph_wo()
        dump("dbg_owo", owo_fm.rearrange("p a b -> p (a b)"))
        P.barrier("tensor", ["act"])
        P.barrier("vector", ["act"])
        P.barrier("gpsimd", ["act"])
        ph_epi()
        dump("dbg_x2", x2.rearrange("p a b -> p (a b)"))
        P.barrier("sync", ["dve", "act"])
        P.barrier("gpsimd", ["dve", "act"])
        ph_tern()
        dump("dbg_rs", rs12[:, :])
        P.barrier("sync", ["g", "dve", "pe"])
        P.barrier("scalar", ["g", "dve"])
        P.barrier("tensor", ["g"])
        ph_ffn()
        P.barrier("gpsimd", ["g"])
        P.run(block)

    return nc


_NC_CACHE = {}


def kernel(L0_attn_x, L0_attn_Wq, L0_attn_bq, L0_attn_Wk, L0_attn_bk,
           L0_attn_Wv, L0_attn_bv, L0_attn_Wo, L0_attn_bo, L0_attn_Wt,
           L0_attn_bt, L0_attn_Wg, L0_attn_bg, L0_ffn_W1, L0_ffn_b1,
           L0_ffn_W2, L0_ffn_b2):
    x = np.ascontiguousarray(np.asarray(L0_attn_x, dtype=np.float32))
    if "nc" not in _NC_CACHE:
        _NC_CACHE["nc"] = build_program()
    nc = _NC_CACHE["nc"]
    common = {
        "Wq": np.ascontiguousarray(L0_attn_Wq, np.float32),
        "Wk": np.ascontiguousarray(L0_attn_Wk, np.float32),
        "Wv": np.ascontiguousarray(L0_attn_Wv, np.float32),
        "Wo": np.ascontiguousarray(L0_attn_Wo, np.float32),
        "Wg": np.ascontiguousarray(L0_attn_Wg, np.float32),
        "bq": np.ascontiguousarray(L0_attn_bq, np.float32),
        "bk": np.ascontiguousarray(L0_attn_bk, np.float32),
        "bv": np.ascontiguousarray(L0_attn_bv, np.float32),
        "bo": np.ascontiguousarray(L0_attn_bo, np.float32),
        "bg": np.ascontiguousarray(L0_attn_bg, np.float32),
        "Wt": np.ascontiguousarray(L0_attn_Wt, np.float32),
        "bt": np.ascontiguousarray(np.asarray(L0_attn_bt, np.float32).reshape(1)),
        "W1": np.ascontiguousarray(L0_ffn_W1, np.float32),
        "b1": np.ascontiguousarray(L0_ffn_b1, np.float32),
        "W2": np.ascontiguousarray(L0_ffn_W2, np.float32),
        "b2": np.ascontiguousarray(L0_ffn_b2, np.float32),
    }
    in_maps = []
    for c in range(8):
        b, j = c // 2, c % 2
        m = dict(common)
        m["xkv"] = np.ascontiguousarray(x[b])
        m["xq"] = np.ascontiguousarray(x[b, j * Q:(j + 1) * Q])
        in_maps.append(m)
    res = run_bass_kernel_spmd(nc, in_maps, core_ids=list(range(8)))
    out = np.empty((B, S, D), np.float32)
    for c in range(8):
        b, j = c // 2, c % 2
        out[b, j * Q:(j + 1) * Q] = res.results[c]["y"]
    return out


# version 22
# speedup vs baseline: 2.3791x; 1.0149x over previous
import numpy as np
import concourse.bass as bass
import concourse.mybir as mybir
from concourse.bass_utils import run_bass_kernel_spmd

F32 = mybir.dt.float32
F16 = mybir.dt.float16
Alu = mybir.AluOpType
Act = mybir.ActivationFunctionType
AX = mybir.AxisListType.X

B, S, D = 4, 4096, 1024
Q = 2048
NHP = 8
ZINIT = 0.67448975
MAGIC = 12582912.0
LN_EPS = 1e-5

ENGINES = ("sync", "vector", "scalar", "tensor", "gpsimd")


class Prog:
    def __init__(self):
        self.streams = {e: [] for e in ENGINES}
        self.sems = {}
        self.handles = {}

    def bind(self, mapping):
        for n, h in mapping.items():
            self.handles[n] = h
            self.sems[n] = 0

    def on(self, eng, fn, wait=(), inc=None, inc_n=1):
        wl = []
        for w in wait:
            if isinstance(w, tuple):
                if w[1] > 0:
                    wl.append((self.handles[w[0]], w[1]))
            elif self.sems[w] > 0:
                wl.append((self.handles[w], self.sems[w]))
        it = None
        if inc is not None:
            self.sems[inc] += inc_n
            it = (self.handles[inc], self.sems[inc] - self.sems[inc] + inc_n)
            it = (self.handles[inc], inc_n)
        self.streams[eng].append((wl, fn, it))

    def cnt(self, name):
        return self.sems[name]

    def now(self, *names):
        return [(n, self.sems[n]) for n in names]

    def barrier(self, eng, names):
        wl = [(self.handles[n], self.sems[n]) for n in names
              if self.sems[n] > 0]
        if wl:
            self.streams[eng].append((wl, None, None))

    def run(self, block):
        def mk(ename):
            def body(eng):
                for wl, fn, it in self.streams[ename]:
                    for h, v in wl:
                        eng.wait_ge(h, v)
                    inst = fn(eng) if fn is not None else None
                    if it is not None:
                        inst.then_inc(it[0], it[1])
            return body
        block.sync(mk("sync"))
        block.vector(mk("vector"))
        block.scalar(mk("scalar"))
        block.tensor(mk("tensor"))
        block.gpsimd(mk("gpsimd"))


def build_program(debug=False):
    nc = bass.Bass("TRN2", target_bir_lowering=False, debug=False)

    xkv = nc.dram_tensor("xkv", [S, D], F32, kind="ExternalInput").ap()
    xq_d = nc.dram_tensor("xq", [Q, D], F32, kind="ExternalInput").ap()
    Wq = nc.dram_tensor("Wq", [D, D], F32, kind="ExternalInput").ap()
    Wk = nc.dram_tensor("Wk", [D, D], F32, kind="ExternalInput").ap()
    Wv = nc.dram_tensor("Wv", [D, D], F32, kind="ExternalInput").ap()
    Wo_d = nc.dram_tensor("Wo", [D, D], F32, kind="ExternalInput").ap()
    Wg_d = nc.dram_tensor("Wg", [D, D], F32, kind="ExternalInput").ap()
    bias_d = {}
    for nm in ("bq", "bk", "bv", "bo", "bg", "b2"):
        bias_d[nm] = nc.dram_tensor(nm, [D], F32, kind="ExternalInput").ap()
    b1_d = nc.dram_tensor("b1", [4 * D], F32, kind="ExternalInput").ap()
    Wt_d = nc.dram_tensor("Wt", [D, 1], F32, kind="ExternalInput").ap()
    bt_d = nc.dram_tensor("bt", [1], F32, kind="ExternalInput").ap()
    W1_d = nc.dram_tensor("W1", [4 * D, D], F32, kind="ExternalInput").ap()
    W2_d = nc.dram_tensor("W2", [D, 4 * D], F32, kind="ExternalInput").ap()
    y_d = nc.dram_tensor("y", [Q, D], F32, kind="ExternalOutput").ap()
    scr_h = nc.dram_tensor("scr", [256], F32, kind="Internal").ap()
    dbg = {}
    if debug:
        for nm, shape, dt_ in (
                ("dbg_xtln", [128, 4096], F16), ("dbg_tchain", [1, 8], F32),
                ("dbg_kn", [128, 4096], F16), ("dbg_qn", [128, 2048], F16),
                ("dbg_vtm", [128, 4224], F16), ("dbg_tb", [128, 4096], F16),
                ("dbg_otmp", [65, 512], F16), ("dbg_oall", [128, 16384], F16),
                ("dbg_owo", [128, 16384], F16), ("dbg_x2", [128, 16384], F32),
                ("dbg_m2h", [64, 64], F16), ("dbg_ksum", [128, 1], F16),
                ("dbg_strow", [1, 512], F32), ("dbg_eg", [128, 512], F16),
                ("dbg_rs", [128, 2], F32), ("dbg_x2b", [128, 8192], F32),
                ("dbg_x2l", [128, 8192], F16), ("dbg_w1s", [128, 1024], F16)):
            dbg[nm] = nc.dram_tensor(nm, shape, dt_, kind="ExternalOutput").ap()
    w1tT_h = nc.dram_tensor("w1tT", [32, 128, 8, 128], F16, kind="Internal").ap()
    w2tT_h = nc.dram_tensor("w2tT", [8, 128, 32, 128], F16, kind="Internal").ap()

    ARN = 101248

    import contextlib
    with contextlib.ExitStack() as _es:
        ident = _es.enter_context(nc.sbuf_tensor("ident", [128, 128], F16))
        ones_r = _es.enter_context(nc.sbuf_tensor("ones_r", [1, 128], F16))
        ones_c = _es.enter_context(nc.sbuf_tensor("ones_c", [128, 1], F16))
        bd2 = _es.enter_context(nc.sbuf_tensor("bd2", [128, 2], F16))
        bd2T = _es.enter_context(nc.sbuf_tensor("bd2T", [2, 128], F16))
        bias5 = _es.enter_context(nc.sbuf_tensor("bias5", [128, 48], F32))
        b1s = _es.enter_context(nc.sbuf_tensor("b1s", [128, 32], F32))
        wtrow = _es.enter_context(nc.sbuf_tensor("wtrow", [1, 1024], F32))
        tmp1 = _es.enter_context(nc.sbuf_tensor("tmp1", [1, 1024], F32))
        small = _es.enter_context(nc.sbuf_tensor("small", [128, 16], F32))
        cst = _es.enter_context(nc.sbuf_tensor("cst", [128, 8], F32))
        ones32 = _es.enter_context(nc.sbuf_tensor("ones32", [1, 128], F32))
        ones_f = _es.enter_context(nc.sbuf_tensor("ones_f", [128, 128], F16))
        tchain = _es.enter_context(nc.sbuf_tensor("tchain", [1, 8], F32))
        rtempb = _es.enter_context(nc.sbuf_tensor("rtempb", [128, 1], F32))
        rs12 = _es.enter_context(nc.sbuf_tensor("rs12", [128, 2], F32))
        accw = _es.enter_context(nc.sbuf_tensor("accw", [128, 2], F32))
        AR = _es.enter_context(nc.sbuf_tensor("AR", [128, ARN], F16))
        ps_a = _es.enter_context(nc.psum_tensor("ps_a", [128, 512], F32))
        ps_b = _es.enter_context(nc.psum_tensor("ps_b", [128, 512], F32))
        ps_c = _es.enter_context(nc.psum_tensor("ps_c", [128, 512], F32))
        ps_d = _es.enter_context(nc.psum_tensor("ps_d", [128, 512], F32))
        ps_av1 = _es.enter_context(nc.psum_tensor("ps_av1", [65, 512], F32))
        ps_av2 = _es.enter_context(nc.psum_tensor("ps_av2", [65, 512], F32))
        ps_t1 = _es.enter_context(nc.psum_tensor("ps_t1", [128, 128], F16))
        ps_t2 = _es.enter_context(nc.psum_tensor("ps_t2", [128, 128], F16))
        s_ld = _es.enter_context(nc.semaphore("s_ld"))
        s_g = _es.enter_context(nc.semaphore("s_g"))
        s_pe = _es.enter_context(nc.semaphore("s_pe"))
        s_act = _es.enter_context(nc.semaphore("s_act"))
        s_dve = _es.enter_context(nc.semaphore("s_dve"))
        s_gp = _es.enter_context(nc.semaphore("s_gp"))
        s_w1 = [_es.enter_context(nc.semaphore(f"s_w1{i}")) for i in range(2)]
        s_w2s = _es.enter_context(nc.semaphore("s_w2s"))
        s_tw = [_es.enter_context(nc.semaphore(f"s_tw{i}")) for i in range(2)]
        block = _es.enter_context(nc.Block())
        P = Prog()
        P.bind({"ld": s_ld, "g": s_g, "pe": s_pe, "act": s_act,
                "dve": s_dve, "gp": s_gp, "w10": s_w1[0], "w11": s_w1[1],
                "w2s": s_w2s, "tw0": s_tw[0], "tw1": s_tw[1]})

        def v2(off, n, p0=0, p1=128):
            assert off + n <= ARN
            return AR[p0:p1, off:off + n]

        def v3(off, a, b, p0=0, p1=128):
            assert off + a * b <= ARN
            return AR[p0:p1, off:off + a * b].rearrange(
                "p (a b) -> p a b", a=a, b=b)

        def vf32(off, n, p0=0, p1=128):
            assert off % 2 == 0 and off + 2 * n <= ARN
            return AR[p0:p1, off:off + 2 * n].bitcast(F32)

        def vf32_3(off, a, b):
            assert off % 2 == 0 and off + 2 * a * b <= ARN
            return AR[:, off:off + 2 * a * b].bitcast(F32).rearrange(
                "p (a b) -> p a b", a=a, b=b)

        o4 = 65536
        xT_LN = v3(0, 8, 4096)
        x2 = vf32_3(0, 16, 1024)
        xT_LNq = v3(32768, 8, 2048)
        owo_fm = v3(32768, 8, 2048)
        O_all = v3(49152, 8, 2048)
        hg = v3(32768, 32, 1024)
        tw32 = [vf32(32768, 8192), vf32(49152, 8192)]

        kbuf = v2(o4 + 0, 4096)
        ktmp = v2(o4 + 4096, 4096)
        ktm = v3(o4 + 8192, 32, 128)
        wsl = v3(o4 + 8192, 24, 128)
        Et = [v2(o4 + 8192 + i * 512, 512) for i in range(4)]
        wb = [v2(o4 + 10240 + i * 512, 512) for i in range(4)]
        mbuf = v2(o4 + 27920, 512)
        drow = vf32(o4 + 28944, 512, 0, 1)
        drow16 = v2(o4 + 29968, 512, 0, 1)
        otmp = v2(o4 + 30480, 512, 0, 65)
        vtm = v3(o4 + 12288, 32, 132)
        qbuf = v2(o4 + 16512, 2048)
        qtmp = v2(o4 + 18560, 2048)
        Tb = v3(o4 + 20608, 2, 2048)
        rk_c = v2(o4 + 24704, 512, 0, 2)
        rq_c = v2(o4 + 25216, 512, 0, 2)
        lnb = vf32(o4 + 25728, 512, 0, 2)
        m2h = [v2(o4 + 26752, 64, 0, 64), v2(o4 + 26816, 64, 64, 128)]
        ksum16 = v2(o4 + 26880, 1)
        Bbuf_h = [v2(o4 + 26896, 512, 0, 64), v2(o4 + 26896, 512, 64, 128)]
        rb65 = v2(o4 + 27408, 512, 0, 65)
        strow = vf32(o4 + 27408, 512)
        sigv = vf32(o4 + 4096, 512)
        musq = vf32(o4 + 8192, 512)
        Trow16 = v2(o4 + 18560, 512)
        avraw = [vf32(o4 + 33568, 512, 0, 65), vf32(o4 + 34592, 512, 0, 65)]

        ep_x32 = [vf32(o4 + 0, 1024), vf32(o4 + 2048, 1024)]
        ep_xf = v2(o4 + 4096, 1024)
        ep_fm = v3(o4 + 5120, 8, 128)
        ep_gl = v2(o4 + 6144, 1024)
        ep_gate = v2(o4 + 7168, 1024)
        ep_ow = v2(o4 + 8192, 1024)
        ep_t1 = v2(o4 + 9216, 1024)
        ep_t2 = vf32(o4 + 10240, 1024)
        wg_sb = v3(o4 + 12288, 8, 1024)
        wo_sb = v3(o4 + 20480, 8, 1024)
        sqt = vf32(o4 + 28672, 1024)
        xin = [vf32(o4 + 12288, 1024), vf32(o4 + 14336, 1024)]
        xf16 = [v2(o4 + 16384, 1024), v2(o4 + 17408, 1024)]
        xln16 = v2(o4 + 18432, 1024)
        f_x2l = v3(o4 + 0, 8, 1024)
        f_lnout = v2(o4 + 8448, 1024)
        f_fmo = v2(o4 + 9472, 1024)
        f_t32 = vf32(o4 + 10496, 128)
        w1stage = [v3(o4 + 31520, 8, 128), v3(o4 + 32544, 8, 128)]
        w2stage = v3(o4 + 0, 32, 128)
        t_coll = v3(o4 + 4096, 32, 128)
        y_stage = vf32_3(o4 + 12288, 8, 1024)
        t_junk = v2(o4 + 12288, 4096)
        t_tr = v2(o4 + 0, 4096)
        t_ftm = v2(o4 + 4096, 128)


        def ln_tile(x32ap, out16ap, wait_extra=()):
            s1 = small[:, 0:1]
            s2 = small[:, 1:2]
            mu = small[:, 2:3]
            m2s = small[:, 3:4]
            rstd = small[:, 4:5]
            w0 = list(wait_extra)
            P.on("vector", lambda v: v.reduce_sum(s1, x32ap, axis=AX),
                 wait=w0 + P.now("act"))
            P.on("scalar", lambda sc: sc.activation(sqt[:, :], x32ap,
                                                    Act.Square),
                 wait=w0 + P.now("dve"), inc="act")
            P.on("vector", lambda v: v.reduce_sum(s2, sqt[:, :], axis=AX),
                 wait=P.now("act"))
            P.on("vector", lambda v: v.tensor_scalar(
                mu, s1, 1.0 / D, None, Alu.mult))
            P.on("vector", lambda v: v.tensor_scalar(
                m2s, s2, 1.0 / D, None, Alu.mult))
            P.on("vector", lambda v: v.tensor_tensor(
                rstd, mu, mu, Alu.mult))
            P.on("vector", lambda v: v.tensor_copy(
                small[:, 15:16], small[:, 14:15]))
            P.on("vector", lambda v: v.tensor_tensor(
                m2s, m2s, rstd, Alu.subtract), inc="dve")
            P.on("scalar", lambda sc: sc.activation(
                rstd, m2s, Act.Ln, bias=cst[:, 0:1]), wait=P.now("dve"))
            P.on("scalar", lambda sc: sc.activation(
                rstd, rstd, Act.Exp, scale=cst[:, 3:4]), inc="act")
            P.on("vector", lambda v: v.tensor_scalar(
                out16ap, x32ap, mu, rstd, Alu.subtract, Alu.mult),
                wait=P.now("act"), inc="dve")

        def bias_sl(i, c):
            return bias5[:, i * 8 + c:i * 8 + c + 1]

        def dump(nm, src_ap):
            if debug and nm in dbg:
                P.on("gpsimd", lambda g, nm=nm, src_ap=src_ap: g.dma_start(
                    out=dbg[nm], in_=src_ap),
                    wait=P.now("dve", "act", "pe", "g", "ld"),
                    inc="g", inc_n=16)

        P.on("gpsimd", lambda g: g.memset(ident[:], 1.0))
        P.on("gpsimd", lambda g: g.affine_select(
            ident[:], ident[:], pattern=[[-1, 128]],
            compare_op=Alu.is_equal, fill=0.0, base=0,
            channel_multiplier=1), inc="gp")
        P.on("vector", lambda v: v.memset(ones_r[:], 1.0))
        P.on("vector", lambda v: v.memset(ones_c[:], 1.0))
        P.on("vector", lambda v: v.memset(bd2[:], 0.0))
        P.on("vector", lambda v: v.memset(bd2[0:64, 0:1], 1.0))
        P.on("vector", lambda v: v.memset(bd2[64:128, 1:2], 1.0))
        P.on("vector", lambda v: v.memset(ones32[:, :], 1.0))
        P.on("vector", lambda v: v.memset(ones_f[:, :], 1.0))
        P.on("vector", lambda v: v.memset(cst[:, 0:1], LN_EPS))
        P.on("vector", lambda v: v.memset(cst[:, 1:2], 1e-24))
        P.on("vector", lambda v: v.memset(cst[:, 2:3], 1e-12))
        P.on("vector", lambda v: v.memset(cst[:, 3:4], -0.5))
        P.on("vector", lambda v: v.memset(cst[:, 4:5], 0.5), inc="dve")
        P.on("tensor", lambda te: te.transpose(
            ps_t1[0:2, :], bd2[:, :], ident[:]),
            wait=P.now("dve") + P.now("gp"), inc="pe")
        P.on("vector", lambda v: v.tensor_copy(bd2T[:, :], ps_t1[0:2, :]),
             wait=P.now("pe"), inc="dve")
        for i, nm in enumerate(("bq", "bk", "bv", "bo", "bg", "b2")):
            for c in range(8):
                P.on("sync", lambda s, nm=nm, i=i, c=c: s.dma_start(
                    out=bias5[:, i * 8 + c:i * 8 + c + 1],
                    in_=bias_d[nm][c * 128:(c + 1) * 128].rearrange("(a b) -> a b", b=1)), inc="ld", inc_n=16)
        for c in range(32):
            P.on("sync", lambda s, c=c: s.dma_start(
                out=b1s[:, c:c + 1],
                in_=b1_d[c * 128:(c + 1) * 128].rearrange("(a b) -> a b", b=1)),
                inc="ld", inc_n=16)
        P.on("sync", lambda s: s.dma_start(
            out=wtrow[:], in_=Wt_d[:, 0].rearrange("(b a) -> b a", b=1)),
            inc="ld", inc_n=16)
        P.on("sync", lambda s: s.dma_start(
            out=tchain[0:1, 0:1], in_=bt_d.rearrange("(b a) -> b a", b=1)),
            inc="ld", inc_n=16)

        ntile = S // 128
        for t in range(ntile + Q // 128):
            is_q = t >= ntile
            ti = t - ntile if is_q else t
            src = xq_d if is_q else xkv
            dstT = xT_LNq if is_q else xT_LN
            bi = t % 2
            P.on("sync", lambda s, src=src, ti=ti, bi=bi: s.dma_start(
                out=xin[bi], in_=src[ti * 128:(ti + 1) * 128, :]),
                wait=P.now("dve"), inc="ld", inc_n=16)
            if not is_q:
                P.on("gpsimd", lambda g, ti=ti, bi=bi: g.dma_start(
                    out=xf16[bi], in_=xkv[ti * 128:(ti + 1) * 128, :]),
                    wait=P.now("pe"), inc="g", inc_n=16)
            ln_tile(xin[bi], xln16[:, :],
                    wait_extra=P.now("ld") + P.now("pe"))
            if not is_q:
                gw = P.now("g")
                for half, ps in ((0, ps_c), (1, ps_d)):
                    P.on("tensor", lambda te, bi=bi, half=half, ps=ps, ti=ti:
                         te.matmul(ps[0:1, :], ones_c[:, :],
                                   xf16[bi][:, half * 512:(half + 1) * 512],
                                   start=(ti == 0), stop=(ti == ntile - 1),
                                   skip_group_check=True),
                         wait=gw, inc="pe")
            dw = P.now("dve")
            for c in range(8):
                pst = ps_t1 if c % 2 == 0 else ps_t2
                P.on("tensor", lambda te, c=c, pst=pst: te.transpose(
                    pst[:], xln16[:, c * 128:(c + 1) * 128], ident[:]),
                    wait=dw + (P.now("dve") if c >= 2 else []), inc="pe")
                P.on("vector", lambda v, c=c, ti=ti, pst=pst, dstT=dstT:
                     v.tensor_copy(dstT[:, c, ti * 128:(ti + 1) * 128],
                                   pst[:]), wait=P.now("pe"), inc="dve")
        P.on("vector", lambda v: v.tensor_copy(tmp1[0:1, 0:512], ps_c[0:1, :]),
             wait=P.now("pe"))
        P.on("vector", lambda v: v.tensor_copy(tmp1[0:1, 512:1024],
                                               ps_d[0:1, :]))
        P.on("vector", lambda v: v.tensor_tensor(
            tmp1[:], tmp1[:], wtrow[:], Alu.mult))
        P.on("vector", lambda v: v.reduce_sum(
            tchain[0:1, 1:2], tmp1[:], axis=AX), inc="dve")
        P.on("vector", lambda v: v.tensor_copy(
            small[:, 15:16], small[:, 14:15]))
        P.on("vector", lambda v: v.tensor_scalar(
            tchain[0:1, 2:3], tchain[0:1, 1:2], 1.0 / S, tchain[0:1, 0:1],
            Alu.mult, Alu.add), wait=P.now("ld"))
        P.on("scalar", lambda sc: sc.activation(
            tchain[0:1, 3:4], tchain[0:1, 2:3], Act.Sigmoid),
            wait=P.now("dve"), inc="act")
        P.on("vector", lambda v: v.tensor_scalar(
            tchain[0:1, 4:5], tchain[0:1, 3:4], 0.5, None, Alu.add),
            wait=P.now("act"))
        P.on("vector", lambda v: v.tensor_copy(
            small[:, 15:16], small[:, 14:15]))
        P.on("vector", lambda v: v.reciprocal(
            tchain[0:1, 5:6], tchain[0:1, 4:5]), inc="dve")
        P.on("tensor", lambda te: te.matmul(
            ps_d[:, 0:1], ones32[:, :], tchain[0:1, 5:6],
            start=True, stop=True), wait=P.now("dve"), inc="pe")
        P.on("vector", lambda v: v.tensor_copy(rtempb[:, :], ps_d[:, 0:1]),
             wait=P.now("pe"), inc="dve")

        dump("dbg_xtln", xT_LN[:, 0, :])
        dump("dbg_tchain", tchain[:, :])
        lastc = {}

        def use_ps(key):
            return [w for w in (lastc.get(key),) if w]

        def took(key, sem):
            lastc[key] = (sem, P.cnt(sem))

        pend_norm = []
        avr_done = {}

        def ph1(hp):
            c0 = hp * 128
            def norm_flush(hp, cur_h2):
                h2, qc, ai = pend_norm.pop(0)
                bps = ps_av2 if cur_h2 == 0 else ps_av1
                P.on("vector", lambda v, ai=ai: v.reciprocal(
                    drow[:, :], avraw[ai][0:1, :]),
                    wait=[("dve", avr_done[(h2, qc)])])
                P.on("vector", lambda v: v.tensor_copy(
                    drow16[:, :], drow[:, :]),
                    wait=use_ps("otmp"), inc="dve")
                P.on("tensor", lambda te, bps=bps: te.matmul(
                    bps[0:65, :], ones_r[:, 0:65], drow16[:, :],
                    start=True, stop=True),
                    wait=P.now("dve") + use_ps(("avb", id(bps))), inc="pe")
                P.on("vector", lambda v, bps=bps: v.tensor_copy(
                    rb65[:, :], bps[0:65, :]), wait=P.now("pe"),
                    inc="dve")
                took(("avb", id(bps)), "dve")
                P.on("vector", lambda v, ai=ai: v.tensor_tensor(
                    otmp[:, :], avraw[ai][:, :], rb65[:, :], Alu.mult),
                    inc="dve")
                took(("avr", ai), "dve")
                P.on("gpsimd", lambda g, hp=hp, h2=h2, qc=qc:
                     g.dma_start(
                         out=O_all[h2 * 64:(h2 + 1) * 64, hp,
                                   qc * 512:(qc + 1) * 512],
                         in_=otmp[1:65, :]),
                     wait=P.now("dve"), inc="g", inc_n=16)
                took("otmp", "g")

            if hp == 0:
                P.on("vector", lambda v: v.memset(vtm[:, :, 0:1], 1.0),
                     wait=P.now("ld", "g", "pe"))
                P.on("vector", lambda v: v.memset(vtm[:, :, 66:67], 1.0),
                     inc="dve")
            for wi, wsrc in ((0, Wq), (1, Wk), (2, Wv)):
                P.on("gpsimd", lambda g, wi=wi, wsrc=wsrc, c0=c0:
                     g.dma_start(
                         out=wsl[:, wi * 8:(wi + 1) * 8, :],
                         in_=wsrc[:, c0:c0 + 128].rearrange(
                             "(dc p) f -> p dc f", p=128)),
                     wait=use_ps("wsl") if wi == 0 else [],
                     inc="g", inc_n=16)
            gw = P.now("g")

            def proj(kind, nsl, wbase, dst_fm, bias_i, xsrc, nchunk):
                pend = []

                def norm_tail(kc):
                    sq_src = ktmp if kind == "k" else qtmp
                    P.on("tensor", lambda te, kc=kc, sq_src=sq_src:
                         te.matmul(
                             ps_c[0:2, :], bd2[:, :],
                             sq_src[:, kc * 512:(kc + 1) * 512],
                             start=True, stop=True),
                         wait=[("act", sq_done[kc])] + use_ps("ps_c2"),
                         inc="pe")
                    P.on("scalar", lambda sc: sc.activation(
                        lnb[:, :], ps_c[0:2, :], Act.Ln,
                        bias=cst[0:2, 1:2]),
                        wait=P.now("pe") + use_ps("lnb"), inc="act")
                    took("ps_c2", "act")
                    rr = rk_c if kind == "k" else rq_c
                    P.on("scalar", lambda sc, rr=rr: sc.activation(
                        rr[:, :], lnb[:, :], Act.Exp,
                        scale=cst[0:2, 3:4]), inc="act")
                    took("lnb", "act")
                    if kind == "q":
                        P.on("vector", lambda v: v.tensor_scalar(
                            rq_c[:, :], rq_c[:, :], rtempb[0:2, 0:1],
                            None, Alu.mult), wait=P.now("act"),
                            inc="dve")
                    P.on("tensor", lambda te, rr=rr: te.matmul(
                        ps_d[:, :], bd2T[:, :], rr[:, :],
                        start=True, stop=True),
                        wait=(P.now("dve") if kind == "q"
                              else P.now("act")) + use_ps("ps_dn"),
                        inc="pe")
                    P.on("vector", lambda v, kc=kc: v.tensor_tensor(
                        dst_fm[:, kc * 512:(kc + 1) * 512],
                        dst_fm[:, kc * 512:(kc + 1) * 512],
                        ps_d[:, :], Alu.mult),
                        wait=P.now("pe"), inc="dve")
                    took("ps_dn", "dve")

                sq_done = {}
                for kc in range(nchunk):
                    ps = ps_a if kc % 2 == 0 else ps_b
                    for dc in range(8):
                        P.on("tensor", lambda te, kc=kc, dc=dc, ps=ps:
                             te.matmul(
                                 ps[:], wsl[:, wbase + dc, :],
                                 xsrc[:, dc, kc * 512:(kc + 1) * 512],
                                 start=(dc == 0), stop=(dc == 7)),
                             wait=((gw if kind == "k" else []) +
                                   use_ps(id(ps)) if dc == 0 else []),
                             inc="pe")
                    pw = P.now("pe")
                    P.on("scalar", lambda sc, kc=kc, ps=ps: sc.activation(
                        dst_fm[:, kc * 512:(kc + 1) * 512], ps[:],
                        Act.Identity, bias=bias_sl(bias_i, hp)),
                        wait=pw, inc="act")
                    if kind != "v":
                        sq_dst = ktmp if kind == "k" else qtmp
                        P.on("scalar", lambda sc, kc=kc, ps=ps, sq_dst=sq_dst:
                             sc.activation(
                                 sq_dst[:, kc * 512:(kc + 1) * 512], ps[:],
                                 Act.Square, bias=bias_sl(bias_i, hp)),
                             inc="act")
                        sq_done[kc] = P.cnt("act")
                    took(id(ps), "act")
                    if kind != "v":
                        pend.append(kc)
                        if len(pend) >= 2:
                            norm_tail(pend.pop(0))
                for kc in pend:
                    norm_tail(kc)

            proj("k", None, 8, kbuf, 1, xT_LN, 8)
            proj("q", None, 0, qbuf, 0, xT_LNq, 4)
            proj("v", None, 16, ktmp, 2, xT_LN, 8)
            took("wsl", "pe")

            for src_buf, kind in ((ktmp, "v"), (kbuf, "kn")):
                for kt in range(32):
                    pst = ps_t1 if kt % 2 == 0 else ps_t2
                    P.on("tensor", lambda te, kt=kt, pst=pst, src_buf=src_buf:
                         te.transpose(
                             pst[:], src_buf[:, kt * 128:(kt + 1) * 128],
                             ident[:]),
                         wait=(P.now("act" if kind == "v" else "dve")
                               if kt == 0 else []) + use_ps(id(pst)),
                         inc="pe")
                    pw = P.now("pe")
                    if kind == "v":
                        P.on("vector", lambda v, kt=kt, pst=pst:
                             v.tensor_copy(vtm[:, kt, 1:65], pst[:, 0:64]),
                             wait=pw)
                        P.on("vector", lambda v, kt=kt, pst=pst:
                             v.tensor_copy(vtm[:, kt, 67:131],
                                           pst[:, 64:128]), inc="dve")
                    else:
                        P.on("vector", lambda v, kt=kt, pst=pst:
                             v.tensor_copy(ktm[:, kt, :], pst[:]),
                             wait=pw, inc="dve")
                    took(id(pst), "dve")

            for h2, psx in ((0, ps_c), (1, ps_d)):
                for kt in range(32):
                    P.on("tensor", lambda te, h2=h2, kt=kt, psx=psx:
                         te.matmul(psx[h2 * 64:(h2 + 1) * 64, 0:64],
                                   ktm[:, kt, h2 * 64:(h2 + 1) * 64],
                                   ktm[:, kt, h2 * 64:(h2 + 1) * 64],
                                   start=(kt == 0), stop=(kt == 31),
                                   tile_position=(0, h2 * 64)),
                         wait=((P.now("dve") + use_ps("ps_c2") +
                                use_ps("ps_dn")) if kt == 0 and h2 == 0
                               else []),
                         inc="pe")
            for kt in range(32):
                P.on("tensor", lambda te, kt=kt: te.matmul(
                    ps_a[:, 0:1], ktm[:, kt, :], ones_c[:, :],
                    start=(kt == 0), stop=(kt == 31)),
                    wait=(use_ps(id(ps_a)) if kt == 0 else []), inc="pe")
            pw = P.now("pe")
            P.on("vector", lambda v: v.tensor_copy(m2h[0][:, :],
                                                   ps_c[0:64, 0:64]), wait=pw)
            P.on("vector", lambda v: v.tensor_copy(m2h[1][:, :],
                                                   ps_d[64:128, 0:64]))
            P.on("vector", lambda v: v.tensor_copy(ksum16[:, :],
                                                   ps_a[:, 0:1]), inc="dve")
            took(id(ps_a), "dve")
            took("ps_c2", "dve")
            took("ps_dn", "dve")

            for h2 in range(2):
                r0 = h2 * 64
                psv = ps_av1 if h2 == 0 else ps_av2
                for qc in range(4):
                    qsl = qbuf[r0:r0 + 64, qc * 512:(qc + 1) * 512]
                    p4 = 32 * qc
                    P.on("tensor", lambda te, h2=h2, r0=r0, qsl=qsl:
                         te.matmul(
                             ps_a[r0:r0 + 64, :], m2h[h2][:, :], qsl,
                             start=True, stop=True, tile_position=(r0, r0)),
                         wait=P.now("dve") + use_ps(id(ps_a)), inc="pe")
                    P.on("vector", lambda v, h2=h2, r0=r0, qsl=qsl:
                         v.tensor_tensor(
                             Bbuf_h[h2][:, :], qsl, ps_a[r0:r0 + 64, :],
                             Alu.mult), wait=P.now("pe"), inc="dve")
                    took(id(ps_a), "dve")
                    P.on("tensor", lambda te, r0=r0, p4=p4, qsl=qsl:
                         te.matmul(
                             ps_b[p4:p4 + 1, :], ksum16[r0:r0 + 64, :], qsl,
                             start=True, stop=True, tile_position=(r0, p4)),
                         wait=P.now("dve") +
                         (use_ps(id(ps_b)) if qc == 0 else []), inc="pe")
                    P.on("tensor", lambda te, h2=h2, r0=r0, p4=p4:
                         te.matmul(
                             ps_c[p4:p4 + 1, :], ones_c[r0:r0 + 64, :],
                             Bbuf_h[h2][:, :],
                             start=True, stop=True, tile_position=(r0, p4)),
                         wait=(use_ps(id(ps_c)) if qc == 0 else []),
                         inc="pe")
                pw = P.now("pe")
                P.on("vector", lambda v: v.tensor_scalar(
                    strow[:, :], ps_b[:, :], 1.0 / S, None, Alu.mult),
                    wait=pw + use_ps("otmp"))
                took(id(ps_b), "dve")
                P.on("vector", lambda v: v.tensor_scalar(
                    sigv[:, :], ps_c[:, :], 1.0 / S, None, Alu.mult),
                    inc="dve")
                took(id(ps_c), "dve")
                P.on("vector", lambda v: v.tensor_tensor(
                    musq[:, :], strow[:, :], strow[:, :], Alu.mult),
                    wait=P.now("dve"))
                P.on("vector", lambda v: v.tensor_tensor(
                    sigv[:, :], sigv[:, :], musq[:, :], Alu.subtract),
                    inc="dve")
                P.on("scalar", lambda sc: sc.activation(
                    sigv[:, :], sigv[:, :], Act.Ln, bias=cst[:, 2:3]),
                    wait=P.now("dve"))
                P.on("scalar", lambda sc: sc.activation(
                    sigv[:, :], sigv[:, :], Act.Exp,
                    scale=cst[:, 4:5]), inc="act")
                P.on("vector", lambda v: v.tensor_scalar(
                    sigv[:, :], sigv[:, :], ZINIT, None, Alu.mult),
                    wait=P.now("act"))
                P.on("vector", lambda v: v.tensor_tensor(
                    strow[:, :], strow[:, :], sigv[:, :], Alu.add),
                    inc="dve")
                P.on("scalar", lambda sc: sc.activation(
                    strow[:, :], strow[:, :], Act.Exp),
                    wait=P.now("dve"), inc="act")
                P.on("vector", lambda v: v.tensor_copy(
                    Trow16[:, :], strow[:, :]), wait=P.now("act"),
                    inc="dve")
                for qc in range(4):
                    P.on("tensor", lambda te, qc=qc: te.matmul(
                        ps_d[:, :], ones_f[32 * qc:32 * qc + 1, :],
                        Trow16[32 * qc:32 * qc + 1, :],
                        start=True, stop=True, tile_position=(32 * qc, 0)),
                        wait=(P.now("dve") if qc == 0 else []) +
                        use_ps(id(ps_d)), inc="pe")
                    P.on("vector", lambda v, h2=h2, qc=qc: v.tensor_copy(
                        Tb[:, h2, qc * 512:(qc + 1) * 512], ps_d[:, :]),
                        wait=P.now("pe"), inc="dve")
                    took(id(ps_d), "dve")

                rings = (ps_a, ps_b, ps_c, ps_d)

                def emit_mm(kt, qc, qsl):
                    pss = rings[kt % 4]
                    P.on("tensor", lambda te, kt=kt, pss=pss, qsl=qsl,
                         r0=r0: te.matmul(
                             pss[:, :],
                             kbuf[r0:r0 + 64, kt * 128:(kt + 1) * 128],
                             qsl, start=True, stop=True,
                             tile_position=(r0, 0)),
                         wait=(P.now("dve") if kt == 0 and qc == 0
                               else []) + use_ps(id(pss)), inc="pe")

                def emit_maskw(kt, qc):
                    pss = rings[kt % 4]
                    ei = kt % 4
                    P.on("scalar", lambda sc, pss=pss, ei=ei:
                         sc.activation(Et[ei][:, :], pss[:, :], Act.Exp),
                         wait=[("pe", pe_after_mm[kt])] +
                         use_ps(("et", ei)), inc="act")
                    took(id(pss), "act")
                    P.on("vector", lambda v, ei=ei, h2=h2, qc=qc:
                         v.tensor_tensor(
                             mbuf[:, :], Et[ei][:, :],
                             Tb[:, h2, qc * 512:(qc + 1) * 512],
                             Alu.is_ge),
                         wait=P.now("act") + use_ps(("wb", ei)))
                    P.on("vector", lambda v, ei=ei: v.tensor_tensor(
                        wb[ei][:, :], mbuf[:, :], Et[ei][:, :],
                        Alu.mult), inc="dve")
                    took(("et", ei), "dve")
                    dve_after_w[kt] = P.cnt("dve")

                def emit_av(kt):
                    ei = kt % 4
                    st_sl = (vtm[:, kt, 0:65] if h2 == 0
                             else vtm[:, kt, 66:131])
                    P.on("tensor", lambda te, kt=kt, psv=psv,
                         st_sl=st_sl: te.matmul(
                             psv[:, :], st_sl, wb[ei][:, :],
                             start=(kt == 0), stop=(kt == 31)),
                         wait=[("dve", dve_after_w[kt])] +
                         (use_ps(("avb", id(psv))) if kt == 0 else []),
                         inc="pe")
                    took(("wb", ei), "pe")

                for qc in range(4):
                    qsl = qbuf[r0:r0 + 64, qc * 512:(qc + 1) * 512]
                    pe_after_mm = {}
                    dve_after_w = {}
                    for kt in range(4):
                        emit_mm(kt, qc, qsl)
                        pe_after_mm[kt] = P.cnt("pe")
                    for kt in range(32):
                        emit_maskw(kt, qc)
                        if kt > 0:
                            emit_av(kt - 1)
                        if kt + 4 < 32:
                            emit_mm(kt + 4, qc, qsl)
                            pe_after_mm[kt + 4] = P.cnt("pe")
                    emit_av(31)
                    ai = (h2 * 4 + qc) % 2
                    P.on("vector", lambda v, psv=psv, ai=ai: v.tensor_copy(
                        avraw[ai][:, :], psv[0:65, :]),
                        wait=P.now("pe") + use_ps(("avr", ai)), inc="dve")
                    avr_done[(h2, qc)] = P.cnt("dve")
                    pend_norm.append((h2, qc, ai))
                    norm_flush(hp, h2)


        def ph_wo():
            for dc in range(8):
                P.on("gpsimd", lambda g, dc=dc: g.dma_start(
                    out=wo_sb[:, dc, :], in_=Wo_d[dc * 128:(dc + 1) * 128, :]),
                    wait=P.now("dve"), inc="g", inc_n=16)
            gw = P.now("g")
            for dc in range(8):
                for qc in range(4):
                    ps = ps_a if (dc * 4 + qc) % 2 == 0 else ps_b
                    for hp in range(NHP):
                        P.on("tensor", lambda te, dc=dc, qc=qc, hp=hp, ps=ps:
                             te.matmul(
                                 ps[:], wo_sb[:, hp, dc * 128:(dc + 1) * 128],
                                 O_all[:, hp, qc * 512:(qc + 1) * 512],
                                 start=(hp == 0), stop=(hp == 7)),
                             wait=(gw if dc == 0 and qc == 0 and hp == 0
                                   else (P.now("act") if hp == 0 else [])),
                             inc="pe")
                    P.on("scalar", lambda sc, dc=dc, qc=qc, ps=ps:
                         sc.activation(
                             owo_fm[:, dc, qc * 512:(qc + 1) * 512], ps[:],
                             Act.Identity, bias=bias_sl(3, dc)),
                         wait=P.now("pe"), inc="act")

        def ph_epi():
            for dc in range(8):
                P.on("gpsimd", lambda g, dc=dc: g.dma_start(
                    out=wg_sb[:, dc, :], in_=Wg_d[dc * 128:(dc + 1) * 128, :]),
                    wait=(P.now("dve") if dc == 0 else []), inc="g",
                    inc_n=16)
            gw = P.now("g")
            for t in range(Q // 128):
                bi = t % 2
                P.on("sync", lambda s, t=t, bi=bi: s.dma_start(
                    out=ep_x32[bi], in_=xq_d[t * 128:(t + 1) * 128, :]),
                    wait=P.now("dve"), inc="ld", inc_n=16)
                ln_tile(ep_x32[bi], ep_xf[:, :],
                        wait_extra=P.now("ld") + (gw if t == 0 else []))
                dw = P.now("dve")
                for c in range(8):
                    pst = ps_t1 if c % 2 == 0 else ps_t2
                    P.on("tensor", lambda te, c=c, pst=pst: te.transpose(
                        pst[:], ep_xf[:, c * 128:(c + 1) * 128], ident[:]),
                        wait=dw + (P.now("dve") if c >= 2 else []), inc="pe")
                    P.on("vector", lambda v, c=c, pst=pst: v.tensor_copy(
                        ep_fm[:, c, :], pst[:]), wait=P.now("pe"),
                        inc="dve")
                dw = P.now("dve")
                for dc in range(8):
                    ps = ps_a if dc % 2 == 0 else ps_b
                    for di in range(8):
                        P.on("tensor", lambda te, dc=dc, di=di, ps=ps:
                             te.matmul(
                                 ps[:, 0:128],
                                 wg_sb[:, di, dc * 128:(dc + 1) * 128],
                                 ep_fm[:, di, :],
                                 start=(di == 0), stop=(di == 7)),
                             wait=((dw + P.now("act")) if di == 0 else []),
                             inc="pe")
                    P.on("scalar", lambda sc, dc=dc, ps=ps: sc.activation(
                        ep_t1[:, 0:128], ps[:, 0:128], Act.Identity,
                        bias=bias_sl(4, dc)), wait=P.now("pe"), inc="act")
                    pst = ps_t1 if dc % 2 == 0 else ps_t2
                    P.on("tensor", lambda te, pst=pst: te.transpose(
                        pst[:], ep_t1[:, 0:128], ident[:]),
                        wait=P.now("act"), inc="pe")
                    P.on("vector", lambda v, dc=dc, pst=pst: v.tensor_copy(
                        ep_gl[:, dc * 128:(dc + 1) * 128], pst[:]),
                        wait=P.now("pe"), inc="dve")
                P.on("scalar", lambda sc: sc.activation(
                    ep_gate[:, :], ep_gl[:, :], Act.Sigmoid),
                    wait=P.now("dve"), inc="act")
                for c in range(8):
                    pst = ps_t1 if c % 2 == 0 else ps_t2
                    P.on("tensor", lambda te, c=c, t=t, pst=pst: te.transpose(
                        pst[:], owo_fm[:, c, t * 128:(t + 1) * 128],
                        ident[:]), wait=P.now("dve"), inc="pe")
                    P.on("vector", lambda v, c=c, pst=pst: v.tensor_copy(
                        ep_ow[:, c * 128:(c + 1) * 128], pst[:]),
                        wait=P.now("pe"), inc="dve")
                P.on("vector", lambda v: v.tensor_tensor(
                    ep_t1[:, :], ep_ow[:, :], ep_xf[:, :], Alu.subtract),
                    wait=P.now("act"))
                P.on("vector", lambda v: v.tensor_tensor(
                    ep_t1[:, :], ep_gate[:, :], ep_t1[:, :], Alu.mult))
                P.on("vector", lambda v: v.tensor_tensor(
                    ep_t1[:, :], ep_t1[:, :], ep_xf[:, :], Alu.add))
                P.on("vector", lambda v: v.tensor_copy(
                    ep_t2[:, :], ep_t1[:, :]))
                P.on("vector", lambda v, t=t, bi=bi: v.tensor_tensor(
                    x2[:, t, :], ep_x32[bi], ep_t2[:, :], Alu.add),
                    inc="dve")

        def ph_tern():
            for wi, (wsrc, nchunk, width) in enumerate(
                    ((W1_d, 32, 1024), (W2_d, 8, 4096))):
                P.on("vector", lambda v, wi=wi: v.memset(
                    accw[:, wi:wi + 1], 0.0), inc="dve")
                for ch in range(nchunk):
                    bi = ch % 2
                    P.on("sync", lambda s, wsrc=wsrc, ch=ch, bi=bi,
                         width=width: s.dma_start(
                             out=tw32[bi][:, 0:width],
                             in_=wsrc[ch * 128:(ch + 1) * 128, :]),
                         wait=use_ps(("tw", bi)), inc=f"tw{bi}", inc_n=16)
                    P.on("scalar", lambda sc, bi=bi, width=width:
                         sc.activation(t_junk[:, 0:width],
                                       tw32[bi][:, 0:width], Act.Abs,
                                       accum_out=small[:, 6:7]),
                         wait=P.now(f"tw{bi}") + P.now("dve"), inc="act")
                    took(("tw", bi), "act")
                    P.on("vector", lambda v: v.tensor_copy(
                        small[:, 15:16], small[:, 14:15]))
                    P.on("vector", lambda v, wi=wi: v.tensor_tensor(
                        accw[:, wi:wi + 1], accw[:, wi:wi + 1],
                        small[:, 6:7], Alu.add), wait=P.now("act"),
                        inc="dve")
                P.on("gpsimd", lambda g, wi=wi: g.dma_start(
                    out=scr_h[wi * 128:(wi + 1) * 128].rearrange(
                        "(a b) -> a b", a=128),
                    in_=accw[:, wi:wi + 1]),
                    wait=P.now("dve"), inc="g", inc_n=16)
                P.on("gpsimd", lambda g, wi=wi: g.dma_start(
                    out=tmp1[0:1, wi * 128:(wi + 1) * 128],
                    in_=scr_h[wi * 128:(wi + 1) * 128].rearrange(
                        "(a b) -> a b", a=1)),
                    wait=P.now("g"), inc="g", inc_n=16)
                P.on("vector", lambda v, wi=wi: v.reduce_sum(
                    tchain[0:1, 6 + wi:7 + wi],
                    tmp1[0:1, wi * 128:(wi + 1) * 128], axis=AX),
                    wait=P.now("g"))
                P.on("vector", lambda v: v.tensor_copy(
                    small[:, 15:16], small[:, 14:15]))
                P.on("vector", lambda v, wi=wi: v.tensor_scalar(
                    tchain[0:1, 6 + wi:7 + wi], tchain[0:1, 6 + wi:7 + wi],
                    1.0 / (4096.0 * 1024.0), 1e-8, Alu.mult, Alu.max))
                P.on("vector", lambda v: v.tensor_copy(
                    small[:, 15:16], small[:, 14:15]))
                P.on("vector", lambda v, wi=wi: v.reciprocal(
                    tchain[0:1, 6 + wi:7 + wi], tchain[0:1, 6 + wi:7 + wi]),
                    inc="dve")
                P.on("tensor", lambda te, wi=wi: te.matmul(
                    ps_d[:, 0:1], ones32[:, :], tchain[0:1, 6 + wi:7 + wi],
                    start=True, stop=True), wait=P.now("dve"), inc="pe")
                P.on("vector", lambda v, wi=wi: v.tensor_copy(
                    rs12[:, wi:wi + 1], ps_d[:, 0:1]),
                    wait=P.now("pe"), inc="dve")
            gw = P.now("g")
            for wi, (wsrc, dsth, nchunk, width) in enumerate(
                    ((W1_d, w1tT_h, 32, 1024), (W2_d, w2tT_h, 8, 4096))):
                for ch in range(nchunk):
                    bi = ch % 2
                    P.on("sync", lambda s, wsrc=wsrc, ch=ch, bi=bi,
                         width=width: s.dma_start(
                             out=tw32[bi][:, 0:width],
                             in_=wsrc[ch * 128:(ch + 1) * 128, :]),
                         wait=(gw if ch < 2 and wi == 0 else []) +
                         use_ps(("tw", bi)), inc=f"tw{bi}", inc_n=16)
                    lw = P.now(f"tw{bi}")
                    P.on("vector", lambda v, bi=bi, wi=wi, width=width:
                         v.tensor_scalar(
                             tw32[bi][:, 0:width], tw32[bi][:, 0:width],
                             rs12[:, wi:wi + 1], MAGIC, Alu.mult, Alu.add),
                         wait=lw)
                    P.on("vector", lambda v, bi=bi, width=width:
                         v.tensor_scalar(
                             tw32[bi][:, 0:width], tw32[bi][:, 0:width],
                             MAGIC, -1.0, Alu.subtract, Alu.max))
                    P.on("vector", lambda v, bi=bi, width=width:
                         v.tensor_scalar(
                             t_tr[:, 0:width], tw32[bi][:, 0:width],
                             1.0, None, Alu.min),
                         wait=use_ps("t_tr"), inc="dve")
                    took(("tw", bi), "dve")
                    dw = P.now("dve")
                    for tt in range(width // 128):
                        pst = ps_t1 if tt % 2 == 0 else ps_t2
                        P.on("tensor", lambda te, tt=tt, pst=pst:
                             te.transpose(pst[:],
                                          t_tr[:, tt * 128:(tt + 1) * 128],
                                          ident[:]),
                             wait=(dw if tt < 2 else []) + use_ps(id(pst)),
                             inc="pe")
                        P.on("vector", lambda v, tt=tt, pst=pst:
                             v.tensor_copy(t_coll[:, tt, :], pst[:]),
                             wait=P.now("pe") + (P.now("g") if tt == 0
                                                 else []), inc="dve")
                        took(id(pst), "dve")
                    took("t_tr", "pe")
                    P.on("gpsimd", lambda g, dsth=dsth, ch=ch, width=width:
                         g.dma_start(
                             out=dsth[ch],
                             in_=t_coll[:, 0:width // 128, :]),
                         wait=P.now("dve"), inc="g", inc_n=16)

        def ph_ffn():
            for tb in range(2):
                P.barrier("vector", ["g", "ld"])
                for tt in range(8):
                    t = tb * 8 + tt
                    ln_tile(x2[:, t, :], f_lnout[:, :],
                            wait_extra=P.now("pe"))
                    dw = P.now("dve")
                    for c in range(8):
                        pst = ps_t1 if c % 2 == 0 else ps_t2
                        P.on("tensor", lambda te, c=c, pst=pst: te.transpose(
                            pst[:], f_lnout[:, c * 128:(c + 1) * 128],
                            ident[:]),
                            wait=dw + (P.now("dve") if c >= 2 else []),
                            inc="pe")
                        P.on("vector", lambda v, c=c, tt=tt, pst=pst:
                             v.tensor_copy(
                                 f_x2l[:, c, tt * 128:(tt + 1) * 128],
                                 pst[:]), wait=P.now("pe"), inc="dve")
                if tb == 1:
                    dump("dbg_eg", f_x2l[:, 0, 0:512])
                    dump("dbg_x2b", x2[:, 8:16, :].rearrange("p a b -> p (a b)"))
                    dump("dbg_x2l", f_x2l.rearrange("p a b -> p (a b)"))
                for fc in range(32):
                    p0 = ps_a if fc % 2 == 0 else ps_b
                    p1 = ps_c if fc % 2 == 0 else ps_d
                    sbi = fc % 2
                    P.on("sync", lambda s, fc=fc, sbi=sbi: s.dma_start(
                        out=w1stage[sbi][:, :, :],
                        in_=w1tT_h[fc]),
                        wait=use_ps(("w1s", sbi)),
                        inc=f"w1{sbi}", inc_n=16)
                    lw = P.now(f"w1{sbi}")
                    for di in range(8):
                        for half, psx in ((0, p0), (1, p1)):
                            P.on("tensor", lambda te, di=di, psx=psx,
                                 half=half, sbi=sbi: te.matmul(
                                     psx[:, :], w1stage[sbi][:, di, :],
                                     f_x2l[:, di,
                                           half * 512:(half + 1) * 512],
                                     start=(di == 0), stop=(di == 7)),
                                 wait=(lw if di == 0 and half == 0 else
                                       []) + (use_ps(id(psx))
                                              if di == 0 else []),
                                 inc="pe")
                    if tb == 1 and fc == 1:
                        dump("dbg_w1s", w1stage[sbi].rearrange(
                            "p a b -> p (a b)"))
                    pw = P.now("pe")
                    for half, psx in ((0, p0), (1, p1)):
                        P.on("scalar", lambda sc, fc=fc, half=half, psx=psx:
                             sc.activation(
                                 hg[:, fc, half * 512:(half + 1) * 512],
                                 psx[:, :], Act.Gelu,
                                 bias=b1s[:, fc:fc + 1]),
                             wait=(pw if half == 0 else []), inc="act")
                        took(id(psx), "act")
                    took(("w1s", sbi), "act")
                if tb == 1:
                    dump("dbg_tb", hg[:, 0:4, :].rearrange("p a b -> p (a b)"))
                for dc in range(8):
                    p0 = ps_a if dc % 2 == 0 else ps_b
                    p1 = ps_c if dc % 2 == 0 else ps_d
                    P.on("sync", lambda s, dc=dc: s.dma_start(
                        out=w2stage[:, :, :],
                        in_=w2tT_h[dc]),
                        wait=use_ps("w2st") + (P.now("act", "dve")
                                               if dc == 0 else []),
                        inc="w2s", inc_n=16)
                    lw = P.now("w2s")
                    for fc in range(32):
                        for half, psx in ((0, p0), (1, p1)):
                            P.on("tensor", lambda te, fc=fc, psx=psx,
                                 half=half: te.matmul(
                                     psx[:, :], w2stage[:, fc, :],
                                     hg[:, fc, half * 512:(half + 1) * 512],
                                     start=(fc == 0), stop=(fc == 31)),
                                 wait=(lw + P.now("act", "dve")
                                       if fc == 0 and half == 0 else
                                       (use_ps(id(psx)) if fc == 0
                                        else [])), inc="pe")
                    took("w2st", "pe")
                    pw = P.now("pe")
                    for half, psx in ((0, p0), (1, p1)):
                        P.on("scalar", lambda sc, dc=dc, half=half, psx=psx:
                             sc.activation(
                                 f_fmo[:, half * 512:(half + 1) * 512],
                                 psx[:, :], Act.Identity,
                                 bias=bias_sl(5, dc)),
                             wait=(pw if half == 0 else []), inc="act")
                        took(id(psx), "act")
                    aw = P.now("act")
                    for tt in range(8):
                        pst = ps_t1 if tt % 2 == 0 else ps_t2
                        P.on("tensor", lambda te, tt=tt, pst=pst:
                             te.transpose(
                                 pst[:], f_fmo[:, tt * 128:(tt + 1) * 128],
                                 ident[:]),
                             wait=aw + (P.now("dve") if tt >= 2 else []),
                             inc="pe")
                        t = tb * 8 + tt
                        P.on("vector", lambda v, pst=pst: v.tensor_copy(
                            f_t32[:, :], pst[:]), wait=P.now("pe"))
                        P.on("vector", lambda v, t=t, tt=tt, dc=dc:
                             v.tensor_tensor(
                                 y_stage[:, tt, dc * 128:(dc + 1) * 128],
                                 x2[:, t, dc * 128:(dc + 1) * 128],
                                 f_t32[:, :], Alu.add), inc="dve")
                dw = P.now("dve")
                for tt in range(8):
                    t = tb * 8 + tt
                    P.on("gpsimd", lambda g, t=t, tt=tt: g.dma_start(
                        out=y_d[t * 128:(t + 1) * 128, :],
                        in_=y_stage[:, tt, :]),
                        wait=dw, inc="g", inc_n=16)

        for hp in range(NHP):
            ph1(hp)
        dump("dbg_oall", O_all.rearrange("p a b -> p (a b)"))
        P.barrier("tensor", ["dve", "act", "g"])
        P.barrier("gpsimd", ["dve", "act", "pe"])
        ph_wo()
        dump("dbg_owo", owo_fm.rearrange("p a b -> p (a b)"))
        P.barrier("tensor", ["act"])
        P.barrier("vector", ["act"])
        P.barrier("gpsimd", ["act"])
        ph_epi()
        dump("dbg_x2", x2.rearrange("p a b -> p (a b)"))
        P.barrier("sync", ["dve", "act"])
        P.barrier("gpsimd", ["dve", "act"])
        ph_tern()
        dump("dbg_rs", rs12[:, :])
        P.barrier("sync", ["g", "dve", "pe"])
        P.barrier("scalar", ["g", "dve"])
        P.barrier("tensor", ["g"])
        ph_ffn()
        P.barrier("gpsimd", ["g"])
        P.run(block)

    return nc


_NC_CACHE = {}


def kernel(L0_attn_x, L0_attn_Wq, L0_attn_bq, L0_attn_Wk, L0_attn_bk,
           L0_attn_Wv, L0_attn_bv, L0_attn_Wo, L0_attn_bo, L0_attn_Wt,
           L0_attn_bt, L0_attn_Wg, L0_attn_bg, L0_ffn_W1, L0_ffn_b1,
           L0_ffn_W2, L0_ffn_b2):
    x = np.ascontiguousarray(np.asarray(L0_attn_x, dtype=np.float32))
    if "nc" not in _NC_CACHE:
        _NC_CACHE["nc"] = build_program()
    nc = _NC_CACHE["nc"]
    common = {
        "Wq": np.ascontiguousarray(L0_attn_Wq, np.float32),
        "Wk": np.ascontiguousarray(L0_attn_Wk, np.float32),
        "Wv": np.ascontiguousarray(L0_attn_Wv, np.float32),
        "Wo": np.ascontiguousarray(L0_attn_Wo, np.float32),
        "Wg": np.ascontiguousarray(L0_attn_Wg, np.float32),
        "bq": np.ascontiguousarray(L0_attn_bq, np.float32),
        "bk": np.ascontiguousarray(L0_attn_bk, np.float32),
        "bv": np.ascontiguousarray(L0_attn_bv, np.float32),
        "bo": np.ascontiguousarray(L0_attn_bo, np.float32),
        "bg": np.ascontiguousarray(L0_attn_bg, np.float32),
        "Wt": np.ascontiguousarray(L0_attn_Wt, np.float32),
        "bt": np.ascontiguousarray(np.asarray(L0_attn_bt, np.float32).reshape(1)),
        "W1": np.ascontiguousarray(L0_ffn_W1, np.float32),
        "b1": np.ascontiguousarray(L0_ffn_b1, np.float32),
        "W2": np.ascontiguousarray(L0_ffn_W2, np.float32),
        "b2": np.ascontiguousarray(L0_ffn_b2, np.float32),
    }
    in_maps = []
    for c in range(8):
        b, j = c // 2, c % 2
        m = dict(common)
        m["xkv"] = np.ascontiguousarray(x[b])
        m["xq"] = np.ascontiguousarray(x[b, j * Q:(j + 1) * Q])
        in_maps.append(m)
    res = run_bass_kernel_spmd(nc, in_maps, core_ids=list(range(8)))
    out = np.empty((B, S, D), np.float32)
    for c in range(8):
        b, j = c // 2, c % 2
        out[b, j * Q:(j + 1) * Q] = res.results[c]["y"]
    return out
